# version 01
# baseline (speedup 1.0000x reference)
import os
import numpy as np
import ml_dtypes
import concourse.bass as bass
import concourse.mybir as mybir
from concourse.bass_utils import run_bass_kernel_spmd

F32 = mybir.dt.float32
BF16 = mybir.dt.bfloat16
AF = mybir.ActivationFunctionType
ALU = mybir.AluOpType
AX = mybir.AxisListType
BF = ml_dtypes.bfloat16

S = 2048
H = 1024
CC = 8
NH = 16
DH = 64
QCA = 512
NQA = 4
QCB = 256
NQB = 8
NE = 16
WINDOW = 256

ENGS = ("t", "v", "s", "g", "y")
ALL_STAGES = ("ln1", "attn_a", "ln2", "ffn1", "ln3", "attn_b", "ln4", "ffn2")


class Rec:

    def __init__(self, nc):
        self.nc = nc
        self.prog = {e: [] for e in ENGS}
        self.cnt = {}
        self.lastwait = {e: {} for e in ENGS}
        self.sems = {}
        self.owner = {}
        self.nbar = 0
        self.pw = [None] * 8
        self.pr = [[] for _ in range(8)]

    def sem(self, name, owner=None):
        self.cnt[name] = 0
        self.owner[name] = owner

    def tok(self, name):
        return (name, self.cnt[name])

    def op(self, eng, fn, waits=(), inc=None, n=1):
        lw = self.lastwait[eng]
        real = []
        for w in waits:
            if w is None:
                continue
            s_, v = w
            if v > 0 and lw.get(s_, 0) < v:
                real.append((s_, v))
                lw[s_] = v
        out_tok = None
        if inc is not None:
            self.cnt[inc] += n
            out_tok = (inc, self.cnt[inc])
        sems = self.sems

        def play(e, real=real, fn=fn, inc=inc, n=n):
            for s_, v in real:
                e.wait_ge(sems[s_], v)
            inst = fn(e)
            if inc is not None:
                if inst is None:
                    e.nop().then_inc(sems[inc], n)
                else:
                    inst.then_inc(sems[inc], n)
        self.prog[eng].append(play)
        return out_tok

    def dma(self, eng, out, in_, sem, waits=()):
        return self.op(eng, lambda e, o=out, i=in_: e.dma_start(out=o, in_=i),
                       waits=waits, inc=sem, n=16)

    def mmw(self, bank, fn, waits=(), start=True, stop=True):
        ws = list(waits)
        if start:
            ws += self.pr[bank]
        tok = self.op("t", fn, ws, inc="mm")
        if start:
            self.pr[bank] = []
        if stop:
            self.pw[bank] = tok
        return tok

    def prd(self, eng, bank, fn, waits=(), inc=None):
        inc = inc or {"v": "vec", "s": "act"}[eng]
        tok = self.op(eng, fn, list(waits) + [self.pw[bank]], inc=inc)
        self.pr[bank].append(tok)
        return tok

    def barrier(self):
        self.nbar += 1
        target = 5 * self.nbar
        for eng in ENGS:
            own = [(n, self.cnt[n]) for n, o in self.owner.items()
                   if o == eng and self.cnt[n] > 0]
            self.op(eng, lambda e: None, waits=own, inc="ph")
        for eng in ENGS:
            self.op(eng, lambda e: None, waits=[("ph", target)])
        return ("ph", target)


def _blocked(wt):
    c, f = wt.shape
    return np.ascontiguousarray(
        wt.reshape(c // 128, 128, f // 128, 128).transpose(2, 1, 0, 3))


def host_prep(ins):
    d = {}
    pm = np.zeros(H, np.int64)
    pr = np.zeros(H, np.int64)
    for h in range(NH):
        for j in range(DH):
            if j < 32:
                pm[64 * h + j] = 64 * h + 2 * j
                pr[64 * h + j] = 64 * h + 2 * j + 1
            else:
                pm[64 * h + j] = 64 * h + 2 * (j - 32) + 1
                pr[64 * h + j] = 64 * h + 2 * (j - 32)
    wq, wk = np.asarray(ins["L0_attn_a_w_q"]), np.asarray(ins["L0_attn_a_w_k"])
    d["aqm"] = _blocked(np.ascontiguousarray(wq[pm].T))
    d["aqr"] = _blocked(np.ascontiguousarray(wq[pr].T))
    d["akm"] = _blocked(np.ascontiguousarray(wk[pm].T))
    d["akr"] = _blocked(np.ascontiguousarray(wk[pr].T))
    d["avT"] = _blocked(np.ascontiguousarray(np.asarray(ins["L0_attn_a_w_v"]).T))
    d["aoT"] = _blocked(np.ascontiguousarray(np.asarray(ins["L0_attn_a_w_o"]).T))
    d["bqT"] = _blocked(np.ascontiguousarray(np.asarray(ins["L0_attn_b_Wq"]).T))
    d["bkT"] = _blocked(np.ascontiguousarray(np.asarray(ins["L0_attn_b_Wk"]).T))
    d["bvT"] = _blocked(np.ascontiguousarray(np.asarray(ins["L0_attn_b_Wv"]).T))
    d["boT"] = _blocked(np.ascontiguousarray(np.asarray(ins["L0_attn_b_Wo"]).T))
    sgb = _blocked(np.ascontiguousarray(np.asarray(ins["L0_ffn_shared_gate_w"]).T))
    sub = _blocked(np.ascontiguousarray(np.asarray(ins["L0_ffn_shared_up_w"]).T))
    d["sgsu"] = np.ascontiguousarray(
        np.concatenate([sgb.reshape(16, 128, 1024), sub.reshape(16, 128, 1024)], axis=2))
    sdw = np.ascontiguousarray(np.asarray(ins["L0_ffn_shared_down_w"]).T)
    d["sd"] = np.ascontiguousarray(
        sdw.reshape(16, 128, 8, 128).transpose(2, 1, 0, 3).reshape(8, 128, 2048))
    gw = np.asarray(ins["L0_ffn_gate_w"])
    uw = np.asarray(ins["L0_ffn_up_w"])
    dw = np.asarray(ins["L0_ffn_down_w"])

    def gu_blk(w):
        return np.ascontiguousarray(
            w.reshape(NE, 8, 128, 2, 128).transpose(0, 2, 3, 1, 4).reshape(NE, 128, 2048))
    d["expgu"] = np.ascontiguousarray(
        np.concatenate([gu_blk(gw), gu_blk(uw)], axis=2))
    d["dwcc"] = np.ascontiguousarray(
        dw.reshape(NE, 2, 128, 8, 128).transpose(3, 2, 0, 1, 4).reshape(8, 128, 4096))
    d["rwT"] = np.ascontiguousarray(
        np.asarray(ins["L0_ffn_router_w"]).T.reshape(8, 128, 16).transpose(1, 0, 2).reshape(128, 128))
    d["cbias"] = np.ascontiguousarray(
        np.tile(np.asarray(ins["L0_ffn_correction_bias"]).astype(np.float32), (128, NE)))
    half = DH // 2
    inv = 10000.0 ** (-(np.arange(half, dtype=np.float64) * 2.0 / DH))
    t = np.arange(S, dtype=np.float64)
    ang = t[None, :] * inv[:, None]
    cos = np.cos(ang).astype(np.float32)
    sin = np.sin(ang).astype(np.float32)
    csA = np.empty((128, S), np.float32)
    csB = np.empty((128, S), np.float32)
    for b in range(4):
        csA[32 * b:32 * b + 32] = cos
        csB[32 * b:32 * b + 32] = (-sin) if (b % 2 == 0) else sin
    d["csA"], d["csB"] = csA, csB
    kk = np.arange(128)[:, None]
    qq = np.arange(QCA)[None, :]
    ma = np.zeros((128, 4 * QCA), BF)
    for r in range(4):
        ma[:, r * QCA:(r + 1) * QCA] = (qq >= kk + 128 * r).astype(BF)
    d["maska"] = ma
    qq2 = np.arange(QCB)[None, :]
    mb = np.zeros((128, 4 * QCB), BF)
    for i, r in enumerate((-2, -1, 0, 1)):
        off = qq2 - kk - 128 * r
        mb[:, i * QCB:(i + 1) * QCB] = ((off >= 0) & (off <= WINDOW)).astype(BF)
    d["maskb"] = mb
    d["identf"] = np.eye(128, dtype=np.float32)
    esel = np.zeros((NE, NE * 128), BF)
    for e in range(NE):
        esel[e, 128 * e:128 * e + 128] = 1.0
    d["esel"] = esel
    return d


def build_nc(stages):
    nc = bass.Bass("TRN2", target_bir_lowering=False)
    ap = {}

    def din(name, shape, dtype=F32):
        ap[name] = nc.dram_tensor(name, list(shape), dtype, kind="ExternalInput").ap()

    din("xin", (CC, 128, S))
    for n in ("aqm", "aqr", "akm", "akr", "avT", "aoT", "bqT", "bkT", "bvT", "boT"):
        din(n, (8, 128, 8, 128))
    din("sgsu", (16, 128, 2048))
    din("sd", (8, 128, 2048))
    din("expgu", (NE, 128, 4096))
    din("dwcc", (8, 128, 4096))
    din("rwT", (128, 128))
    din("cbias", (128, 256))
    din("csA", (128, S))
    din("csB", (128, S))
    din("maska", (128, 4 * QCA), BF16)
    din("maskb", (128, 4 * QCB), BF16)
    din("identf", (128, 128))
    din("esel", (NE, NE * 128), BF16)
    yout = nc.dram_tensor("y", [CC, 128, S], F32, kind="ExternalOutput").ap()

    sc = {}
    need_ffn = any(st.startswith("ffn") for st in stages)
    if need_ffn:
        sc["sgsu"] = nc.dram_tensor("sgsu_bf", [16, 128, 2048], BF16, kind="Internal").ap()
        sc["sd"] = nc.dram_tensor("sd_bf", [8, 128, 2048], BF16, kind="Internal").ap()
        sc["expgu"] = nc.dram_tensor("expgu_bf", [NE, 128, 4096], BF16, kind="Internal").ap()
        sc["dwcc"] = nc.dram_tensor("dwcc_bf", [8, 128, 4096], BF16, kind="Internal").ap()

    sb = {}

    def sbt(name, shape, dtype, off=None):
        if off is None:
            sb[name] = nc.alloc_sbuf_tensor(name, list(shape), dtype)
        else:
            sb[name] = nc.alloc_sbuf_tensor_at(name, list(shape), dtype, offset=off)
        return sb[name]

    sbt("x", (128, CC * S), F32)
    sbt("xn", (128, CC * S), BF16)
    sbt("ones_bf", (128, 128), BF16)
    sbt("eps_sb", (128, 1), F32)
    sbt("onesf", (128, 128), F32)
    sbt("identf_sb", (128, 128), F32)
    arena = nc.sbuf_base
    top = nc.sbuf_top
    A = [arena]
    ph_id = [0]

    def ar(name, shape, dtype, align=64):
        nbytes = int(np.prod(shape[1:])) * (2 if dtype == BF16 else 4)
        off = (A[0] + align - 1) // align * align
        assert off + nbytes <= top, f"SBUF arena overflow at {name}: {off+nbytes-top} over"
        A[0] = off + nbytes
        return sbt(f"{name}_{ph_id[0]}", shape, dtype, off=off)

    def ar_at(name, shape, dtype, off):
        return sbt(f"{name}_{ph_id[0]}", shape, dtype, off=off)

    def ar_reset():
        A[0] = arena
        ph_id[0] += 1

    P = [nc.alloc_psum_tensor(f"P{i}", [128, 512], F32) for i in range(8)]

    k = Rec(nc)
    k.sem("mm", "t")
    k.sem("act", "s")
    k.sem("vec", "v")
    k.sem("ph", None)
    for s_ in ("dxin", "dcst", "dst", "fs0", "fs1", "fd0", "fd1", "fe0", "fe1",
               "fw0", "fw1"):
        k.sem(s_, "y")
    for s_ in ("gcst", "fcst"):
        k.sem(s_, "g")
    for i in range(12):
        k.sem(f"aw{i}", "g")

    ident_tok = [None]

    def emit_load():
        for ccx in range(CC):
            k.dma("y", sb["x"][:, ccx * S:(ccx + 1) * S], ap["xin"][ccx], "dxin")
        ident_tok[0] = k.dma("y", sb["identf_sb"][:], ap["identf"], "dcst")
        k.op("v", lambda e: e.memset(sb["ones_bf"][:], 1.0), inc="vec")
        k.op("v", lambda e: e.memset(sb["onesf"][:], 1.0), inc="vec")
        k.op("v", lambda e: e.memset(sb["eps_sb"][:], 1e-5), inc="vec")
        if need_ffn:
            k.dma("g", sc["sgsu"], ap["sgsu"], "fcst")
            k.dma("g", sc["sd"], ap["sd"], "fcst")
            for e2 in range(4):
                k.dma("g", sc["expgu"][4 * e2:4 * e2 + 4], ap["expgu"][4 * e2:4 * e2 + 4], "fcst")
            k.dma("g", sc["dwcc"], ap["dwcc"], "fcst")

    def emit_ln():
        lnbf = ar("lnbf", (128, 2 * QCA), BF16)
        lnsq = ar("lnsq", (128, 2 * QCA), BF16)
        m1 = ar("m1", (128, QCA), F32)
        m2 = ar("m2", (128, QCA), F32)
        m3 = ar("m3", (128, QCA), F32)
        m4 = ar("m4", (128, QCA), F32)
        xin_tok = k.tok("dxin")
        prev_last = None
        sq_war = {0: None, 1: None}
        for qc in range(NQA):
            q0 = qc * QCA
            for ccx in range(CC):
                slot = ccx % 2
                war = ("mm", k.cnt["mm"]) if (ccx >= 2 or qc > 0) else None
                tb = k.op("s", lambda e, ccx=ccx, slot=slot, q0=q0:
                          e.copy(lnbf[:, slot * QCA:(slot + 1) * QCA],
                                 sb["x"][:, ccx * S + q0:ccx * S + q0 + QCA]),
                          waits=[xin_tok, war, sq_war[slot]], inc="act")
                tq = k.op("v", lambda e, slot=slot:
                          e.tensor_mul(lnsq[:, slot * QCA:(slot + 1) * QCA],
                                       lnbf[:, slot * QCA:(slot + 1) * QCA],
                                       lnbf[:, slot * QCA:(slot + 1) * QCA]),
                          waits=[tb, war], inc="vec")
                sq_war[slot] = tq
                k.mmw(0, lambda e, slot=slot, ccx=ccx:
                      e.matmul(P[0][:], sb["ones_bf"][:],
                               lnbf[:, slot * QCA:(slot + 1) * QCA],
                               start=(ccx == 0), stop=(ccx == CC - 1)),
                      waits=[tb], start=(ccx == 0), stop=(ccx == CC - 1))
                k.mmw(1, lambda e, slot=slot, ccx=ccx:
                      e.matmul(P[1][:], sb["ones_bf"][:],
                               lnsq[:, slot * QCA:(slot + 1) * QCA],
                               start=(ccx == 0), stop=(ccx == CC - 1)),
                      waits=[tq], start=(ccx == 0), stop=(ccx == CC - 1))
            t1 = k.prd("s", 0, lambda e: e.mul(m1[:], P[0][:], 1.0 / H), waits=[prev_last])
            t2 = k.prd("s", 1, lambda e: e.mul(m2[:], P[1][:], 1.0 / H), waits=[prev_last])
            t3 = k.op("v", lambda e: e.tensor_mul(m3[:], m1[:], m1[:]),
                      waits=[t1, prev_last], inc="vec")
            t4 = k.op("v", lambda e: e.tensor_sub(m2[:], m2[:], m3[:]), waits=[t2, t3], inc="vec")
            t5 = k.op("s", lambda e: e.activation(m3[:], m2[:], AF.Ln, bias=sb["eps_sb"][:, 0:1]),
                      waits=[t4], inc="act")
            t6 = k.op("s", lambda e: e.activation(m2[:], m3[:], AF.Exp, scale=-0.5),
                      waits=[t5], inc="act")
            t7 = k.op("v", lambda e: e.tensor_mul(m3[:], m1[:], m2[:]), waits=[t6], inc="vec")
            lastc = prev_last
            for ccx in range(CC):
                t8 = k.op("v", lambda e, ccx=ccx, q0=q0:
                          e.tensor_mul(m4[:], sb["x"][:, ccx * S + q0:ccx * S + q0 + QCA], m2[:]),
                          waits=[t6, lastc], inc="vec")
                lastc = k.op("v", lambda e, ccx=ccx, q0=q0:
                             e.tensor_sub(sb["xn"][:, ccx * S + q0:ccx * S + q0 + QCA],
                                          m4[:], m3[:]),
                             waits=[t7, t8], inc="vec")
            last = lastc
            prev_last = last
        return last

    def emit_attn(mode):
        LV = int(os.environ.get("ATTN_LV", "6"))
        a = mode == "a"
        QC = QCA if a else QCB
        NQ = NQA if a else NQB
        if a:
            csA = ar("csA", (128, S), F32)
            csB = ar("csB", (128, S), F32)
            mask = ar("maska", (128, 4 * QCA), BF16)
        else:
            mask = ar("maskb", (128, 4 * QCB), BF16)
        Qh = ar("Qh", (128, S), BF16)
        Kh = ar("Kh", (128, S), BF16)
        VT = ar("VT", (128, S), BF16)
        aout = ar("aout", (128, CC * S), BF16)
        nsl = 6 if a else 4
        wsl = [ar(f"w{i}", (128, 1024), BF16) for i in range(2 * nsl)]
        ebuf = ar("ebuf", (128, 4 * QC), BF16)
        embuf = ar("embuf", (128, 4 * QC), BF16)
        rp1 = ar("rp1", (128, QCA), F32)
        rp2 = ar("rp2", (128, QCA), F32)
        qsq = ar("qsq", (128, QCA), BF16)
        ksq = ar("ksq", (128, QCA), BF16)
        nsb = ar("nsb", (128, QC), F32)
        dinv = ar("dinv", (33, 2 * QC), BF16)
        nacc = ar("nacc", (33, 16), F32)
        nred = ar("nred", (33, 4), F32)
        bias_sb = ar("bias_sb", (128, 2), F32)
        cst = []
        if a:
            cst.append(k.dma("y", csA[:], ap["csA"], "dcst"))
            cst.append(k.dma("y", csB[:], ap["csB"], "dcst"))
            cst.append(k.dma("y", mask[:], ap["maska"], "dcst"))
        else:
            cst.append(k.dma("y", mask[:], ap["maskb"], "dcst"))
        cst_tok = cst[-1]

        mats = (["aqm", "aqr", "akm", "akr", "avT", "aoT"] if a
                else ["bqT", "bkT", "bvT", "boT"])
        nm = 4 if a else 2
        iv = nm
        nw = nsl - 1
        wtok = {}
        wfree = {}

        def loadw(mi, c):
            slot = (c % 2) * nsl + mi
            t = k.dma("g", wsl[slot][:], ap[mats[mi]][c].rearrange("p a b -> p (a b)"),
                      f"aw{slot}", waits=[wfree.get(slot)])
            wtok[(mi, c)] = (t, slot)

        for mi in range(nsl - 1):
            loadw(mi, 0)

        last_xn = k.tok("vec")
        last_ln_act = k.tok("act")
        dv_tok = None
        prev_norm = None
        prev_rope = None
        prev_bc = {0: None, 1: None}
        for c in range(CC):
            if c + 1 < CC:
                for mi in range(nsl - 1):
                    loadw(mi, c + 1)
            rope_toks = []
            for qc in range(NQA):
                q0 = qc * QCA
                pm = []
                if LV == 11:
                    continue
                for mi in range(nm):
                    wt, slot = wtok[(mi, c)]
                    wb = wsl[slot]
                    for ccx in range(CC):
                        tk = k.mmw(mi, lambda e, mi=mi, ccx=ccx, q0=q0, wb=wb:
                                   e.matmul(P[mi][:], wb[:, ccx * 128:ccx * 128 + 128],
                                            sb["xn"][:, ccx * S + q0:ccx * S + q0 + QCA],
                                            start=(ccx == 0), stop=(ccx == CC - 1)),
                                   waits=[wt, last_xn, last_ln_act],
                                   start=(ccx == 0), stop=(ccx == CC - 1))
                    pm.append(tk)
                    wfree[slot] = tk
                iq, ik = (0, 2) if a else (0, 1)
                if LV == 12:
                    continue
                tqs = k.prd("s", iq, lambda e: e.square(qsq[:], P[iq][:]),
                            waits=[prev_norm])
                tks = k.prd("s", ik, lambda e, ik=ik: e.square(ksq[:], P[ik][:]),
                            waits=[prev_norm])
                if LV == 13:
                    continue
                if a:
                    t1 = k.prd("v", 0, lambda e, q0=q0: e.tensor_mul(
                        rp1[:], P[0][:], csA[:, q0:q0 + QCA]),
                        waits=[cst_tok, prev_rope, tqs])
                    t2 = k.prd("v", 1, lambda e, q0=q0: e.tensor_mul(
                        rp2[:], P[1][:], csB[:, q0:q0 + QCA]))
                    tq = k.op("v", lambda e, q0=q0: e.tensor_add(
                        Qh[:, q0:q0 + QCA], rp1[:], rp2[:]), waits=[t1, t2], inc="vec")
                    t1 = k.prd("v", 2, lambda e, q0=q0: e.tensor_mul(
                        rp1[:], P[2][:], csA[:, q0:q0 + QCA]), waits=[tq, tks])
                    t2 = k.prd("v", 3, lambda e, q0=q0: e.tensor_mul(
                        rp2[:], P[3][:], csB[:, q0:q0 + QCA]))
                    tk2 = k.op("v", lambda e, q0=q0: e.tensor_add(
                        Kh[:, q0:q0 + QCA], rp1[:], rp2[:]), waits=[t1, t2], inc="vec")
                else:
                    tq = k.prd("s", 0, lambda e, q0=q0: e.copy(Qh[:, q0:q0 + QCA], P[0][:]))
                    tk2 = k.prd("s", 1, lambda e, q0=q0: e.copy(Kh[:, q0:q0 + QCA], P[1][:]))
                rope_toks += [tq, tk2]
                prev_rope = tk2
                if LV == 14:
                    continue
                for hh in range(2):
                    k.mmw(4, lambda e, hh=hh: e.matmul(
                        P[4][32 * hh:32 * hh + 1, :],
                        sb["ones_bf"][64 * hh:64 * hh + 64, 0:1],
                        qsq[64 * hh:64 * hh + 64, :], start=True, stop=True,
                        tile_position=(64 * hh, 32 * hh)),
                        waits=[tqs], start=(hh == 0), stop=(hh == 1))
                    prev_norm = k.mmw(5, lambda e, hh=hh: e.matmul(
                        P[5][32 * hh:32 * hh + 1, :],
                        sb["ones_bf"][64 * hh:64 * hh + 64, 0:1],
                        ksq[64 * hh:64 * hh + 64, :], start=True, stop=True,
                        tile_position=(64 * hh, 32 * hh)),
                        waits=[tks], start=(hh == 0), stop=(hh == 1))
                for hh in range(2):
                    rw = 32 * hh
                    k.prd("v", 4, lambda e, qc=qc, rw=rw: e.tensor_reduce(
                        nacc[rw:rw + 1, qc:qc + 1], P[4][rw:rw + 1, :], AX.X, ALU.max))
                    rq = k.prd("v", 5, lambda e, qc=qc, rw=rw: e.tensor_reduce(
                        nacc[rw:rw + 1, 4 + qc:5 + qc], P[5][rw:rw + 1, :], AX.X, ALU.max))
            if 11 <= LV <= 15:
                continue
            rr4 = rq
            for hh in range(2):
                rw = 32 * hh
                rr1 = k.op("v", lambda e, rw=rw: e.tensor_reduce(
                    nred[rw:rw + 1, 0:1], nacc[rw:rw + 1, 0:4], AX.X, ALU.max),
                    waits=[rr4], inc="vec")
                rr2 = k.op("v", lambda e, rw=rw: e.tensor_reduce(
                    nred[rw:rw + 1, 1:2], nacc[rw:rw + 1, 4:8], AX.X, ALU.max),
                    waits=[rr1], inc="vec")
                rr3 = k.op("v", lambda e, rw=rw: e.tensor_add(
                    nred[rw:rw + 1, 2:3], nred[rw:rw + 1, 0:1], nred[rw:rw + 1, 1:2]),
                    waits=[rr2], inc="vec")
                rr4 = k.op("v", lambda e, rw=rw: e.tensor_scalar_mul(
                    nred[rw:rw + 1, 3:4], nred[rw:rw + 1, 2:3], -1.0 / 16.0),
                    waits=[rr3], inc="vec")
            for hh in range(2):
                k.mmw(6, lambda e, hh=hh: e.matmul(
                    P[6][:, hh:hh + 1], sb["onesf"][32 * hh:32 * hh + 1, 0:128],
                    nred[32 * hh:32 * hh + 1, 3:4], start=True, stop=True),
                    waits=[rr4], start=(hh == 0), stop=(hh == 1))
            tbias = k.prd("s", 6, lambda e: e.copy(bias_sb[:], P[6][:, 0:2]))
            if LV < 2:
                continue
            wt_v, slot_v = wtok[(iv, c)]
            wb_v = wsl[slot_v]
            vt_toks = []
            for tt in range(16):
                bnk = tt % 4
                for ccx in range(CC):
                    k.mmw(bnk, lambda e, tt=tt, ccx=ccx, wb_v=wb_v, bnk=bnk:
                          e.matmul(P[bnk][:, 0:128],
                                   sb["xn"][:, ccx * S + tt * 128:ccx * S + tt * 128 + 128],
                                   wb_v[:, ccx * 128:ccx * 128 + 128],
                                   start=(ccx == 0), stop=(ccx == CC - 1)),
                          waits=[wt_v, last_xn, last_ln_act],
                          start=(ccx == 0), stop=(ccx == CC - 1))
                ev = k.prd("s", bnk, lambda e, tt=tt, bnk=bnk: e.copy(
                    VT[:, tt * 128:tt * 128 + 128], P[bnk][:, 0:128]))
                vt_toks.append(ev)
            wfree[slot_v] = vt_toks[-1]
            vt_done = vt_toks[-1]

            if LV < 3:
                continue
            econs = {}
            emcons = {}
            for qc in range(NQ):
                q0 = qc * QC
                if a:
                    kts = list(range(16))
                    pv_lo, pv_hi = 0, 4 * qc + 4
                    str_lo = 4 * qc
                else:
                    kts = [t_ for t_ in range(2 * qc - 2, 2 * qc + 2) if t_ >= 0]
                    pv_lo, pv_hi = kts[0], kts[-1] + 1
                    str_lo = -10**9
                pv_cnt = {0: 0, 1: 0}
                d_cnt = {0: 0, 1: 0}
                npv = pv_hi - pv_lo
                for kt_i, kt in enumerate(kts):
                    for hh in range(2):
                        bnk = 2 * hh + kt_i % 2
                        smm = k.mmw(bnk, lambda e, hh=hh, kt=kt, q0=q0, bnk=bnk:
                                    e.matmul(P[bnk][:, 0:QC],
                                             Kh[64 * hh:64 * hh + 64, kt * 128:kt * 128 + 128],
                                             Qh[64 * hh:64 * hh + 64, q0:q0 + QC],
                                             start=True, stop=True),
                                    waits=rope_toks[-2:] + [dv_tok])
                        eb = ebuf[:, bnk * QC:(bnk + 1) * QC]
                        texp = k.prd("s", bnk, lambda e, bnk=bnk, eb=eb, hh=hh:
                                     e.activation(eb, P[bnk][:, 0:QC], AF.Exp,
                                                  bias=bias_sb[:, hh:hh + 1], scale=0.125),
                                     waits=[tbias, econs.get((hh, kt_i % 2))])
                        if LV < 4:
                            continue
                        use_mask = ((a and str_lo <= kt < pv_hi) or (not a))
                        pv_use = eb
                        tmask = texp
                        if use_mask and pv_lo <= kt < pv_hi:
                            mb2 = embuf[:, bnk * QC:(bnk + 1) * QC]
                            r = (kt - 4 * qc) if a else (kt - 2 * qc + 2)
                            tmask = k.op("v", lambda e, mb2=mb2, eb=eb, r=r:
                                         e.tensor_mul(mb2, eb, mask[:, r * QC:r * QC + QC]),
                                         waits=[texp, cst_tok, emcons.get((hh, kt_i % 2))],
                                         inc="vec")
                            pv_use = mb2
                        d_src = eb if a else pv_use
                        d_dep = texp if a else tmask
                        dmm = k.mmw(6 + hh, lambda e, hh=hh, d_src=d_src,
                                    st=(d_cnt[hh] == 0), sp=(d_cnt[hh] == len(kts) - 1):
                                    e.matmul(P[6 + hh][32 * hh:32 * hh + 1, 0:QC],
                                             sb["ones_bf"][:, 0:1], d_src,
                                             start=st, stop=sp,
                                             tile_position=(0, 32 * hh)),
                                    waits=[d_dep],
                                    start=(d_cnt[hh] == 0), stop=(d_cnt[hh] == len(kts) - 1))
                        if a:
                            econs[(hh, kt_i % 2)] = dmm
                        else:
                            econs[(hh, kt_i % 2)] = tmask
                        d_cnt[hh] += 1
                        if pv_lo <= kt < pv_hi:
                            pvmm = k.mmw(4 + hh, lambda e, hh=hh, kt=kt, pv_use=pv_use,
                                         st=(pv_cnt[hh] == 0), sp=(pv_cnt[hh] == npv - 1):
                                         e.matmul(P[4 + hh][64 * hh:64 * hh + 64, 0:QC],
                                                  VT[:, kt * 128 + 64 * hh:kt * 128 + 64 * hh + 64],
                                                  pv_use, start=st, stop=sp,
                                                  tile_position=(0, 64 * hh)),
                                         waits=[tmask, vt_done],
                                         start=(pv_cnt[hh] == 0), stop=(pv_cnt[hh] == npv - 1))
                            pv_cnt[hh] += 1
                            if pv_use is eb:
                                econs[(hh, kt_i % 2)] = pvmm
                            else:
                                emcons[(hh, kt_i % 2)] = pvmm
                if LV < 5:
                    continue
                dr = []
                def _recip(e, hh):
                    with nc.allow_low_precision(reason="bf16 inverse-denominator, margin-checked"):
                        return e.reciprocal(
                            dinv[32 * hh:32 * hh + 1, hh * QC:hh * QC + QC],
                            P[6 + hh][32 * hh:32 * hh + 1, 0:QC])
                for hh in range(2):
                    dr.append(k.prd("v", 6 + hh, lambda e, hh=hh: _recip(e, hh),
                                    waits=[prev_bc[hh]]))
                for hh in range(2):
                    prev_bc[hh] = k.mmw(0, lambda e, hh=hh: e.matmul(
                        P[0][64 * hh:64 * hh + 64, 0:QC],
                        sb["ones_bf"][32 * hh:32 * hh + 1, 0:64],
                        dinv[32 * hh:32 * hh + 1, hh * QC:hh * QC + QC],
                        start=True, stop=True, tile_position=(32 * hh, 64 * hh)),
                        waits=[dr[hh]], start=(hh == 0), stop=(hh == 1))
                ne1 = k.prd("s", 4, lambda e: e.copy(nsb[0:64, 0:QC], P[4][0:64, 0:QC]),
                            waits=[dv_tok])
                ne2 = k.prd("s", 5, lambda e: e.copy(nsb[64:128, 0:QC], P[5][64:128, 0:QC]),
                            waits=[dv_tok])
                dv_tok = k.prd("v", 0, lambda e, c=c, q0=q0: e.tensor_mul(
                    aout[:, c * S + q0:c * S + q0 + QC], nsb[:, 0:QC], P[0][:, 0:QC]),
                    waits=[ne1, ne2])
        last_aout = dv_tok
        if LV < 6:
            return k.tok("vec")

        for fc in range(CC):
            slot = (fc % 2) * nsl + nw
            t_wo = k.dma("g", wsl[slot][:], ap[mats[nw]][fc].rearrange("p a b -> p (a b)"),
                         f"aw{slot}", waits=[wfree.get(slot)])
            for qc in range(NQA):
                q0 = qc * QCA
                pp = 2 + (fc * NQA + qc) % 2
                for dc in range(CC):
                    k.mmw(pp, lambda e, dc=dc, q0=q0, slot=slot, pp=pp:
                          e.matmul(P[pp][:], wsl[slot][:, dc * 128:dc * 128 + 128],
                                   aout[:, dc * S + q0:dc * S + q0 + QCA],
                                   start=(dc == 0), stop=(dc == CC - 1)),
                          waits=[t_wo, last_aout],
                          start=(dc == 0), stop=(dc == CC - 1))
                tadd = k.prd("v", pp, lambda e, fc=fc, q0=q0, pp=pp: e.tensor_add(
                    sb["x"][:, fc * S + q0:fc * S + q0 + QCA],
                    sb["x"][:, fc * S + q0:fc * S + q0 + QCA], P[pp][:]))
            wfree[slot] = tadd
        return tadd

    def emit_ffn():
        FLV = int(os.environ.get("FFN_LV", "5"))
        esel_sb = ar("esel", (NE, NE * 128), BF16)
        cbias_sb = ar("cbias", (128, 256), F32)
        rw_sb = ar("rw", (128, 128), BF16)
        comb_fm = ar("comb_fm", (NE, S), BF16)
        hh = ar("hh", (128, 48 * QCA), BF16)
        hh_off = int(hh.handle.offset) if hasattr(hh, "handle") else None
        hh_base = A[0] - 48 * QCA * 2
        sco_fm = ar_at("sco_fm", (NE, S), F32, hh_base)
        scT = ar_at("scT", (128, 256), F32, hh_base + 8192)
        sbT = ar_at("sbT", (128, 256), F32, hh_base + 8192 + 1024)
        msk = ar_at("msk", (128, 256), F32, hh_base + 8192 + 2048)
        cnt = ar_at("cnt", (128, 256), F32, hh_base + 8192 + 3072)
        ctm = ar_at("ctm", (128, 256), F32, hh_base + 8192 + 4096)
        rb = hh_base + 8192 + 5120
        gs = ar_at("gs", (128, 64), F32, rb)
        gct = ar_at("gct", (128, 64), F32, rb + 256)
        gtm = ar_at("gtm", (128, 64), F32, rb + 512)
        gm = ar_at("gm", (128, 64), F32, rb + 768)
        s1 = ar_at("s1", (128, 64), F32, rb + 1024)
        s2 = ar_at("s2", (128, 64), F32, rb + 1280)
        s3 = ar_at("s3", (128, 64), F32, rb + 1536)
        wsum = ar_at("wsum", (128, 16), F32, rb + 1792)
        comb_tm = ar_at("comb_tm", (128, 256), F32, rb + 1856)
        silu_t = [ar(f"silu_t{i}", (128, QCA), BF16) for i in range(2)]
        t2b = [ar(f"t2b{i}", (128, QCA), BF16) for i in range(2)]
        sgsu_sl = [ar(f"sgsu{i}", (128, 2048), BF16) for i in range(2)]
        sd_sl = [ar(f"sd{i}", (128, 2048), BF16) for i in range(2)]
        dw_sl = [ar(f"dw{i}", (128, 4096), BF16) for i in range(2)]
        eg_sl = [ar(f"eg{i}", (128, 4096), BF16) for i in range(2)]

        k.dma("y", esel_sb[:], ap["esel"], "dcst")
        tcs = k.dma("y", cbias_sb[:], ap["cbias"], "dcst")
        trw = k.dma("g", rw_sb[:], ap["rwT"], "gcst")
        cast_done = k.tok("fcst")
        last_xn = k.tok("vec")
        last_ln_act = k.tok("act")

        for qc in range(NQA):
            q0 = qc * QCA
            for ccx in range(CC):
                k.mmw(6, lambda e, ccx=ccx, q0=q0:
                      e.matmul(P[6][0:16, :], rw_sb[:, ccx * 16:ccx * 16 + 16],
                               sb["xn"][:, ccx * S + q0:ccx * S + q0 + QCA],
                               start=(ccx == 0), stop=(ccx == CC - 1)),
                      waits=[trw, last_xn, last_ln_act],
                      start=(ccx == 0), stop=(ccx == CC - 1))
            tsig = k.prd("s", 6, lambda e, q0=q0: e.activation(
                sco_fm[0:16, q0:q0 + QCA], P[6][0:16, :], AF.Sigmoid))
        for tt in range(16):
            k.mmw(7, lambda e, tt=tt: e.transpose(
                P[7][:, tt * 16:tt * 16 + 16], sco_fm[0:16, tt * 128:tt * 128 + 128],
                sb["identf_sb"][0:16, 0:16]),
                waits=[tsig, ident_tok[0]], start=(tt == 0), stop=(tt == 15))
        v0 = k.prd("v", 7, lambda e: e.tensor_copy(scT[:], P[7][:, 0:256]))
        vt = k.op("v", lambda e: e.tensor_add(sbT[:], scT[:], cbias_sb[:]),
                  waits=[v0, tcs], inc="vec")

        def re(apx):
            return apx.rearrange("p (t e) -> p t e", e=16)
        sb4 = sbT[:].rearrange("p (t g j) -> p t g j", g=4, j=4)
        g4 = gs[:].rearrange("p (t g) -> p t g", g=4)
        s1r = s1[:].rearrange("p (t g) -> p t g", g=4)
        s2r = s2[:].rearrange("p (t g) -> p t g", g=4)
        s3r = s3[:].rearrange("p (t g) -> p t g", g=4)
        gmr = gm[:].rearrange("p (t g) -> p t g", g=4)
        gcr = gct[:].rearrange("p (t g) -> p t g", g=4)
        gtr = gtm[:].rearrange("p (t g) -> p t g", g=4)

        def vop(fn):
            nonlocal vt
            vt = k.op("v", fn, waits=[vt], inc="vec")
            return vt
        vop(lambda e: e.tensor_tensor(s1r, sb4[:, :, :, 0], sb4[:, :, :, 1], ALU.max))
        vop(lambda e: e.tensor_tensor(s2r, sb4[:, :, :, 0], sb4[:, :, :, 1], ALU.min))
        vop(lambda e: e.tensor_tensor(s3r, sb4[:, :, :, 2], sb4[:, :, :, 3], ALU.max))
        vop(lambda e: e.tensor_tensor(gmr, sb4[:, :, :, 2], sb4[:, :, :, 3], ALU.min))
        vop(lambda e: e.tensor_tensor(gcr, s1r, s3r, ALU.max))
        vop(lambda e: e.tensor_tensor(s1r, s1r, s3r, ALU.min))
        vop(lambda e: e.tensor_tensor(s2r, s2r, gmr, ALU.max))
        vop(lambda e: e.tensor_tensor(s1r, s1r, s2r, ALU.max))
        vop(lambda e: e.tensor_tensor(g4, gcr, s1r, ALU.add))
        vop(lambda e: e.memset(gct[:], 0.0))
        for dd in range(1, 4):
            Aap = g4[:, :, 0:4 - dd]
            Bap = g4[:, :, dd:4]
            cA = gcr[:, :, 0:4 - dd]
            cB = gcr[:, :, dd:4]
            tmp = gtr[:, :, 0:4 - dd]
            vop(lambda e, A=Aap, B=Bap, t=tmp: e.tensor_tensor(t, A, B, ALU.is_ge))
            vop(lambda e, t=tmp, c2=cB: e.tensor_tensor(c2, c2, t, ALU.add))
            vop(lambda e, A=Aap, B=Bap, t=tmp: e.tensor_tensor(t, B, A, ALU.is_gt))
            vop(lambda e, t=tmp, c2=cA: e.tensor_tensor(c2, c2, t, ALU.add))
        vop(lambda e: e.tensor_single_scalar(gm[:], gct[:], 2.0, ALU.is_lt))
        mr4 = msk[:].rearrange("p (t g j) -> p t g j", g=4, j=4)
        for j in range(4):
            vop(lambda e, j=j: e.scalar_tensor_tensor(
                mr4[:, :, :, j], sb4[:, :, :, j], 4.0, gmr, ALU.add, ALU.mult))
        vop(lambda e: e.memset(cnt[:], 0.0))
        mre = re(msk[:])
        cre = re(cnt[:])
        tre = re(ctm[:])
        for dd in range(1, 16):
            Aap = mre[:, :, 0:16 - dd]
            Bap = mre[:, :, dd:16]
            cA = cre[:, :, 0:16 - dd]
            cB = cre[:, :, dd:16]
            tmp = tre[:, :, 0:16 - dd]
            vop(lambda e, A=Aap, B=Bap, t=tmp: e.tensor_tensor(t, A, B, ALU.is_ge))
            vop(lambda e, t=tmp, c2=cB: e.tensor_tensor(c2, c2, t, ALU.add))
            vop(lambda e, A=Aap, B=Bap, t=tmp: e.tensor_tensor(t, B, A, ALU.is_gt))
            vop(lambda e, t=tmp, c2=cA: e.tensor_tensor(c2, c2, t, ALU.add))
        vop(lambda e: e.tensor_single_scalar(msk[:], cnt[:], 4.0, ALU.is_lt))
        vop(lambda e: e.tensor_mul(ctm[:], scT[:], msk[:]))
        vop(lambda e: e.tensor_reduce(wsum[:], re(ctm[:]), AX.X, ALU.add))
        vop(lambda e: e.tensor_scalar_add(wsum[:], wsum[:], 1e-20))
        vop(lambda e: e.reciprocal(wsum[:], wsum[:]))
        winb = wsum[:].rearrange("p (t e) -> p t e", e=1).broadcast_to((128, 16, 16))
        vop(lambda e: e.tensor_tensor(re(comb_tm[:]), re(ctm[:]), winb, ALU.mult))
        cevac = None
        for b4 in range(4):
            for j4 in range(4):
                tt = 4 * b4 + j4
                k.mmw(7, lambda e, tt=tt, j4=j4: e.transpose(
                    P[7][0:16, j4 * 128:j4 * 128 + 128],
                    comb_tm[:, tt * 16:tt * 16 + 16], sb["identf_sb"][:]),
                    waits=[vt], start=(j4 == 0), stop=(j4 == 3))
            cevac = k.prd("s", 7, lambda e, b4=b4: e.copy(
                comb_fm[0:16, b4 * 512:b4 * 512 + 512], P[7][0:16, 0:512]))
        comb_done = cevac

        sl_free = {0: None, 1: None}
        ed_free = {0: None, 1: None}
        dwf = {0: None, 1: None}
        sdf = {0: None, 1: None}
        st_free = {0: None, 1: None}
        t2_free = {0: None, 1: None}
        hh_tok = {}
        down_done = None
        res_tok = None
        for qc in range(NQA):
            q0 = qc * QCA
            if FLV < 2:
                continue
            for hc in range(16):
                sl = hc % 2
                tl = k.dma("y", sgsu_sl[sl][:], sc["sgsu"][hc], f"fs{sl}",
                           waits=[cast_done, sl_free[sl]])
                pp = hc % 2
                for ccx in range(CC):
                    k.mmw(pp, lambda e, ccx=ccx, q0=q0, sl=sl, pp=pp:
                          e.matmul(P[pp][:], sgsu_sl[sl][:, ccx * 128:ccx * 128 + 128],
                                   sb["xn"][:, ccx * S + q0:ccx * S + q0 + QCA],
                                   start=(ccx == 0), stop=(ccx == CC - 1)),
                          waits=[tl, last_xn, last_ln_act],
                          start=(ccx == 0), stop=(ccx == CC - 1))
                for ccx in range(CC):
                    tu = k.mmw(2 + pp, lambda e, ccx=ccx, q0=q0, sl=sl, pp=pp:
                               e.matmul(P[2 + pp][:],
                                        sgsu_sl[sl][:, 1024 + ccx * 128:1024 + ccx * 128 + 128],
                                        sb["xn"][:, ccx * S + q0:ccx * S + q0 + QCA],
                                        start=(ccx == 0), stop=(ccx == CC - 1)),
                               waits=[tl],
                               start=(ccx == 0), stop=(ccx == CC - 1))
                ts = k.prd("s", pp, lambda e, pp=pp: e.activation(
                    silu_t[pp][:], P[pp][:], AF.Silu), waits=[st_free[pp]])
                th = k.prd("v", 2 + pp, lambda e, hc=hc, pp=pp: e.tensor_mul(
                    hh[:, hc * QCA:hc * QCA + QCA], silu_t[pp][:], P[2 + pp][:]),
                    waits=[ts, down_done])
                st_free[pp] = th
                hh_tok[hc] = th
                sl_free[sl] = tu
            for ee in range(NE):
                if FLV < 3:
                    break
                sl = ee % 2
                tl = k.dma("y", eg_sl[sl][:], sc["expgu"][ee], f"fe{sl}",
                           waits=[cast_done, ed_free[sl]])
                tcb = k.mmw(4, lambda e, ee=ee, q0=q0: e.matmul(
                    P[4][:], esel_sb[:, ee * 128:ee * 128 + 128],
                    comb_fm[0:16, q0:q0 + QCA], start=True, stop=True),
                    waits=[comb_done])
                for h2 in range(2):
                    pp = h2
                    for ccx in range(CC):
                        k.mmw(pp, lambda e, ccx=ccx, q0=q0, sl=sl, h2=h2, pp=pp:
                              e.matmul(P[pp][:],
                                       eg_sl[sl][:, (h2 * 8 + ccx) * 128:(h2 * 8 + ccx) * 128 + 128],
                                       sb["xn"][:, ccx * S + q0:ccx * S + q0 + QCA],
                                       start=(ccx == 0), stop=(ccx == CC - 1)),
                              waits=[tl], start=(ccx == 0), stop=(ccx == CC - 1))
                    for ccx in range(CC):
                        tu = k.mmw(2 + pp, lambda e, ccx=ccx, q0=q0, sl=sl, h2=h2, pp=pp:
                                   e.matmul(P[2 + pp][:],
                                            eg_sl[sl][:, 2048 + (h2 * 8 + ccx) * 128:2048 + (h2 * 8 + ccx) * 128 + 128],
                                            sb["xn"][:, ccx * S + q0:ccx * S + q0 + QCA],
                                            start=(ccx == 0), stop=(ccx == CC - 1)),
                                   waits=[tl], start=(ccx == 0), stop=(ccx == CC - 1))
                    ts = k.prd("s", pp, lambda e, pp=pp: e.activation(
                        silu_t[pp][:], P[pp][:], AF.Silu), waits=[st_free[pp]])
                    tm = k.prd("v", 2 + pp, lambda e, pp=pp: e.tensor_mul(
                        t2b[pp][:], silu_t[pp][:], P[2 + pp][:]),
                        waits=[ts, t2_free[pp]])
                    st_free[pp] = tm
                    th = k.prd("v", 4, lambda e, ee=ee, h2=h2, pp=pp: e.tensor_mul(
                        hh[:, (16 + 2 * ee + h2) * QCA:(16 + 2 * ee + h2 + 1) * QCA],
                        t2b[pp][:], P[4][:]), waits=[tm, down_done])
                    t2_free[pp] = th
                    hh_tok[16 + 2 * ee + h2] = th
                ed_free[sl] = tu
            for ccx in range(CC):
                if FLV < 4:
                    break
                sl = ccx % 2
                tsd = k.dma("y", sd_sl[sl][:], sc["sd"][ccx], f"fd{sl}",
                            waits=[cast_done, sdf[sl]])
                tdw = k.dma("y", dw_sl[sl][:], sc["dwcc"][ccx], f"fw{sl}",
                            waits=[cast_done, dwf[sl]])
                pp = 5 + ccx % 2
                shstop = FLV < 5
                for hc in range(16):
                    tk = k.mmw(pp, lambda e, hc=hc, sl=sl, pp=pp,
                               sp=(shstop and hc == 15):
                               e.matmul(P[pp][:], sd_sl[sl][:, hc * 128:hc * 128 + 128],
                                        hh[:, hc * QCA:hc * QCA + QCA],
                                        start=(hc == 0), stop=sp),
                               waits=[tsd, hh_tok[hc]],
                               start=(hc == 0), stop=(shstop and hc == 15))
                for ee in range(NE):
                    if FLV < 5:
                        break
                    for h2 in range(2):
                        idx = 2 * ee + h2
                        tk = k.mmw(pp, lambda e, idx=idx, sl=sl, pp=pp, sp=(idx == 31):
                                   e.matmul(P[pp][:],
                                            dw_sl[sl][:, idx * 128:idx * 128 + 128],
                                            hh[:, (16 + idx) * QCA:(16 + idx + 1) * QCA],
                                            start=False, stop=sp),
                                   waits=[tdw, hh_tok[16 + idx]],
                                   start=False, stop=(idx == 31))
                def _res(e, ccx=ccx, q0=q0, pp=pp, qc=qc):
                    inst = e.tensor_add(
                        sb["x"][:, ccx * S + q0:ccx * S + q0 + QCA],
                        sb["x"][:, ccx * S + q0:ccx * S + q0 + QCA], P[pp][:])
                    inst.annotate(f"res qc{qc} ccx{ccx} pp{pp}")
                    return inst
                res_tok = k.prd("v", pp, _res)
                sdf[sl] = tk
                dwf[sl] = tk
                down_done = res_tok
        return res_tok

    def emit_store():
        for ccx in range(CC):
            k.dma("y", yout[ccx], sb["x"][:, ccx * S:(ccx + 1) * S], "dst")
        k.op("y", lambda e: None, waits=[("dst", 16 * CC)])

    emit_load()
    k.barrier()
    for st in stages:
        ar_reset()
        if st.startswith("ln"):
            emit_ln()
        elif st == "attn_a":
            emit_attn("a")
        elif st == "attn_b":
            emit_attn("b")
        else:
            emit_ffn()
        k.barrier()
    emit_store()

    for name in list(k.cnt):
        k.sems[name] = nc.alloc_semaphore(name)

    with nc.Block() as block:
        @block.tensor
        def _(e):
            for f in k.prog["t"]:
                f(e)

        @block.vector
        def _(e):
            for f in k.prog["v"]:
                f(e)

        @block.scalar
        def _(e):
            for f in k.prog["s"]:
                f(e)

        @block.gpsimd
        def _(e):
            for f in k.prog["g"]:
                f(e)

        @block.sync
        def _(e):
            for f in k.prog["y"]:
                f(e)
    return nc


def kernel(**inputs):
    return _run(inputs, list(ALL_STAGES))


def _run(inputs, stages, cores=None):
    hp = host_prep(inputs)
    x = np.asarray(inputs["L0_attn_a_x"])
    B = x.shape[0]
    if cores is None:
        cores = list(range(B))
    nc = build_nc(tuple(stages))
    in_maps = []
    for b in cores:
        m = dict(hp)
        m["xin"] = np.ascontiguousarray(x[b].T.reshape(CC, 128, S))
        in_maps.append(m)
    res = run_bass_kernel_spmd(nc, in_maps, core_ids=list(range(len(cores))))
    outs = []
    for r in res.results:
        y = np.asarray(r["y"]).reshape(H, S)
        outs.append(np.ascontiguousarray(y.T))
    return np.stack(outs, axis=0)


# version 02
# speedup vs baseline: 1.3711x; 1.3711x over previous
import os
import numpy as np
import ml_dtypes
import concourse.bass as bass
import concourse.mybir as mybir
from concourse.bass_utils import run_bass_kernel_spmd

F32 = mybir.dt.float32
BF16 = mybir.dt.bfloat16
AF = mybir.ActivationFunctionType
ALU = mybir.AluOpType
AX = mybir.AxisListType
BF = ml_dtypes.bfloat16

S = 2048
H = 1024
CC = 8
NH = 16
DH = 64
QCA = 512
NQA = 4
QCB = 256
NQB = 8
NE = 16
WINDOW = 256

ENGS = ("t", "v", "s", "g", "y")
ALL_STAGES = ("ln1", "attn_a", "ln2", "ffn1", "ln3", "attn_b", "ln4", "ffn2")


class Rec:

    def __init__(self, nc):
        self.nc = nc
        self.prog = {e: [] for e in ENGS}
        self.cnt = {}
        self.lastwait = {e: {} for e in ENGS}
        self.sems = {}
        self.owner = {}
        self.nbar = 0
        self.pw = [None] * 8
        self.pr = [[] for _ in range(8)]

    def sem(self, name, owner=None):
        self.cnt[name] = 0
        self.owner[name] = owner

    def tok(self, name):
        return (name, self.cnt[name])

    def op(self, eng, fn, waits=(), inc=None, n=1):
        lw = self.lastwait[eng]
        real = []
        for w in waits:
            if w is None:
                continue
            s_, v = w
            if v > 0 and lw.get(s_, 0) < v:
                real.append((s_, v))
                lw[s_] = v
        out_tok = None
        if inc is not None:
            self.cnt[inc] += n
            out_tok = (inc, self.cnt[inc])
        sems = self.sems

        def play(e, real=real, fn=fn, inc=inc, n=n):
            for s_, v in real:
                e.wait_ge(sems[s_], v)
            inst = fn(e)
            if inc is not None:
                if inst is None:
                    e.nop().then_inc(sems[inc], n)
                else:
                    inst.then_inc(sems[inc], n)
        self.prog[eng].append(play)
        return out_tok

    def dma(self, eng, out, in_, sem, waits=()):
        return self.op(eng, lambda e, o=out, i=in_: e.dma_start(out=o, in_=i),
                       waits=waits, inc=sem, n=16)

    def mmw(self, bank, fn, waits=(), start=True, stop=True, nowait=False):
        ws = list(waits)
        if start and not nowait:
            ws += self.pr[bank]
        tok = self.op("t", fn, ws, inc="mm")
        if start:
            self.pr[bank] = []
        if stop:
            self.pw[bank] = tok
        return tok

    def takepr(self, bank):
        r = self.pr[bank]
        self.pr[bank] = []
        return list(r)

    def twait(self, waits):
        self.op("t", lambda e: None, waits=waits)

    def prd(self, eng, bank, fn, waits=(), inc=None):
        inc = inc or {"v": "vec", "s": "act"}[eng]
        tok = self.op(eng, fn, list(waits) + [self.pw[bank]], inc=inc)
        self.pr[bank].append(tok)
        return tok

    def barrier(self):
        self.nbar += 1
        target = 5 * self.nbar
        for eng in ENGS:
            own = [(n, self.cnt[n]) for n, o in self.owner.items()
                   if o == eng and self.cnt[n] > 0]
            self.op(eng, lambda e: None, waits=own, inc="ph")
        for eng in ENGS:
            self.op(eng, lambda e: None, waits=[("ph", target)])
        return ("ph", target)


def _blocked(wt):
    c, f = wt.shape
    return np.ascontiguousarray(
        wt.reshape(c // 128, 128, f // 128, 128).transpose(2, 1, 0, 3))


def host_prep(ins):
    d = {}
    pm = np.zeros(H, np.int64)
    pr = np.zeros(H, np.int64)
    for h in range(NH):
        for j in range(DH):
            if j < 32:
                pm[64 * h + j] = 64 * h + 2 * j
                pr[64 * h + j] = 64 * h + 2 * j + 1
            else:
                pm[64 * h + j] = 64 * h + 2 * (j - 32) + 1
                pr[64 * h + j] = 64 * h + 2 * (j - 32)
    wq, wk = np.asarray(ins["L0_attn_a_w_q"]), np.asarray(ins["L0_attn_a_w_k"])
    d["aqm"] = _blocked(np.ascontiguousarray(wq[pm].T))
    d["aqr"] = _blocked(np.ascontiguousarray(wq[pr].T))
    d["akm"] = _blocked(np.ascontiguousarray(wk[pm].T))
    d["akr"] = _blocked(np.ascontiguousarray(wk[pr].T))
    d["avT"] = _blocked(np.ascontiguousarray(np.asarray(ins["L0_attn_a_w_v"]).T))
    d["aoT"] = _blocked(np.ascontiguousarray(np.asarray(ins["L0_attn_a_w_o"]).T))
    d["bqT"] = _blocked(np.ascontiguousarray(np.asarray(ins["L0_attn_b_Wq"]).T))
    d["bkT"] = _blocked(np.ascontiguousarray(np.asarray(ins["L0_attn_b_Wk"]).T))
    d["bvT"] = _blocked(np.ascontiguousarray(np.asarray(ins["L0_attn_b_Wv"]).T))
    d["boT"] = _blocked(np.ascontiguousarray(np.asarray(ins["L0_attn_b_Wo"]).T))
    sgb = _blocked(np.ascontiguousarray(np.asarray(ins["L0_ffn_shared_gate_w"]).T))
    sub = _blocked(np.ascontiguousarray(np.asarray(ins["L0_ffn_shared_up_w"]).T))
    d["sgsu"] = np.ascontiguousarray(
        np.concatenate([sgb.reshape(16, 128, 1024), sub.reshape(16, 128, 1024)], axis=2))
    sdw = np.ascontiguousarray(np.asarray(ins["L0_ffn_shared_down_w"]).T)
    d["sd"] = np.ascontiguousarray(
        sdw.reshape(16, 128, 8, 128).transpose(2, 1, 0, 3).reshape(8, 128, 2048))
    gw = np.asarray(ins["L0_ffn_gate_w"])
    uw = np.asarray(ins["L0_ffn_up_w"])
    dw = np.asarray(ins["L0_ffn_down_w"])

    def gu_blk(w):
        return np.ascontiguousarray(
            w.reshape(NE, 8, 128, 2, 128).transpose(0, 2, 3, 1, 4).reshape(NE, 128, 2048))
    d["expgu"] = np.ascontiguousarray(
        np.concatenate([gu_blk(gw), gu_blk(uw)], axis=2))
    d["dwcc"] = np.ascontiguousarray(
        dw.reshape(NE, 2, 128, 8, 128).transpose(3, 2, 0, 1, 4).reshape(8, 128, 4096))
    d["rwT"] = np.ascontiguousarray(
        np.asarray(ins["L0_ffn_router_w"]).T.reshape(8, 128, 16).transpose(1, 0, 2).reshape(128, 128))
    d["cbias"] = np.ascontiguousarray(
        np.tile(np.asarray(ins["L0_ffn_correction_bias"]).astype(np.float32), (128, NE)))
    half = DH // 2
    inv = 10000.0 ** (-(np.arange(half, dtype=np.float64) * 2.0 / DH))
    t = np.arange(S, dtype=np.float64)
    ang = t[None, :] * inv[:, None]
    cos = np.cos(ang).astype(np.float32)
    sin = np.sin(ang).astype(np.float32)
    csA = np.empty((128, S), np.float32)
    csB = np.empty((128, S), np.float32)
    for b in range(4):
        csA[32 * b:32 * b + 32] = cos
        csB[32 * b:32 * b + 32] = (-sin) if (b % 2 == 0) else sin
    d["csA"], d["csB"] = csA, csB
    kk = np.arange(128)[:, None]
    qq = np.arange(QCA)[None, :]
    ma = np.zeros((128, 4 * QCA), BF)
    for r in range(4):
        ma[:, r * QCA:(r + 1) * QCA] = (qq >= kk + 128 * r).astype(BF)
    d["maska"] = ma
    qq2 = np.arange(QCB)[None, :]
    mb = np.zeros((128, 4 * QCB), BF)
    for i, r in enumerate((-2, -1, 0, 1)):
        off = qq2 - kk - 128 * r
        mb[:, i * QCB:(i + 1) * QCB] = ((off >= 0) & (off <= WINDOW)).astype(BF)
    d["maskb"] = mb
    d["identf"] = np.eye(128, dtype=np.float32)
    esel = np.zeros((NE, NE * 128), BF)
    for e in range(NE):
        esel[e, 128 * e:128 * e + 128] = 1.0
    d["esel"] = esel
    return d


def build_nc(stages):
    nc = bass.Bass("TRN2", target_bir_lowering=False)
    ap = {}

    def din(name, shape, dtype=F32):
        ap[name] = nc.dram_tensor(name, list(shape), dtype, kind="ExternalInput").ap()

    din("xin", (CC, 128, S))
    for n in ("aqm", "aqr", "akm", "akr", "avT", "aoT", "bqT", "bkT", "bvT", "boT"):
        din(n, (8, 128, 8, 128))
    din("sgsu", (16, 128, 2048))
    din("sd", (8, 128, 2048))
    din("expgu", (NE, 128, 4096))
    din("dwcc", (8, 128, 4096))
    din("rwT", (128, 128))
    din("cbias", (128, 256))
    din("csA", (128, S))
    din("csB", (128, S))
    din("maska", (128, 4 * QCA), BF16)
    din("maskb", (128, 4 * QCB), BF16)
    din("identf", (128, 128))
    din("esel", (NE, NE * 128), BF16)
    yout = nc.dram_tensor("y", [CC, 128, S], F32, kind="ExternalOutput").ap()

    sc = {}
    scw = {}
    need_attn = any(st.startswith("attn") for st in stages)
    if need_attn:
        for n in ("aqm", "aqr", "akm", "akr", "avT", "aoT", "bqT", "bkT", "bvT", "boT"):
            scw[n] = nc.dram_tensor(f"{n}_bf", [8, 128, 1024], BF16, kind="Internal").ap()
    need_ffn = any(st.startswith("ffn") for st in stages)
    if need_ffn:
        sc["sgsu"] = nc.dram_tensor("sgsu_bf", [16, 128, 2048], BF16, kind="Internal").ap()
        sc["sd"] = nc.dram_tensor("sd_bf", [8, 128, 2048], BF16, kind="Internal").ap()
        sc["expgu"] = nc.dram_tensor("expgu_bf", [NE, 128, 4096], BF16, kind="Internal").ap()
        sc["dwcc"] = nc.dram_tensor("dwcc_bf", [8, 128, 4096], BF16, kind="Internal").ap()

    sb = {}

    def sbt(name, shape, dtype, off=None):
        if off is None:
            sb[name] = nc.alloc_sbuf_tensor(name, list(shape), dtype)
        else:
            sb[name] = nc.alloc_sbuf_tensor_at(name, list(shape), dtype, offset=off)
        return sb[name]

    sbt("x", (128, CC * S), F32)
    sbt("xn", (128, CC * S), BF16)
    sbt("ones_bf", (128, 128), BF16)
    sbt("eps_sb", (128, 1), F32)
    sbt("onesf", (128, 128), F32)
    sbt("identf_sb", (128, 128), F32)
    arena = nc.sbuf_base
    top = nc.sbuf_top
    A = [arena]
    ph_id = [0]

    def ar(name, shape, dtype, align=64):
        nbytes = int(np.prod(shape[1:])) * (2 if dtype == BF16 else 4)
        off = (A[0] + align - 1) // align * align
        assert off + nbytes <= top, f"SBUF arena overflow at {name}: {off+nbytes-top} over"
        A[0] = off + nbytes
        return sbt(f"{name}_{ph_id[0]}", shape, dtype, off=off)

    def ar_at(name, shape, dtype, off):
        return sbt(f"{name}_{ph_id[0]}", shape, dtype, off=off)

    def ar_reset():
        A[0] = arena
        ph_id[0] += 1

    P = [nc.alloc_psum_tensor(f"P{i}", [128, 512], F32) for i in range(8)]

    k = Rec(nc)
    k.sem("mm", "t")
    k.sem("act", "s")
    k.sem("vec", "v")
    k.sem("ph", None)
    for s_ in ("dxin", "dcst", "dst", "fs0", "fs1", "fd0", "fd1", "fe0", "fe1",
               "fw0", "fw1"):
        k.sem(s_, "y")
    for s_ in ("gcst", "fcst", "awcast"):
        k.sem(s_, "g")
    for i in range(12):
        k.sem(f"aw{i}", "y")
    for i in (20, 21):
        k.sem(f"aw{i}", "y")

    ident_tok = [None]

    def emit_load():
        for ccx in range(CC):
            k.dma("y", sb["x"][:, ccx * S:(ccx + 1) * S], ap["xin"][ccx], "dxin")
        ident_tok[0] = k.dma("y", sb["identf_sb"][:], ap["identf"], "dcst")
        k.op("v", lambda e: e.memset(sb["ones_bf"][:], 1.0), inc="vec")
        k.op("v", lambda e: e.memset(sb["onesf"][:], 1.0), inc="vec")
        k.op("v", lambda e: e.memset(sb["eps_sb"][:], 1e-5), inc="vec")
        if need_attn:
            for n in scw:
                k.dma("g", scw[n], ap[n].rearrange("a p c j -> a p (c j)"), "awcast")
        if need_ffn:
            k.dma("g", sc["sgsu"], ap["sgsu"], "fcst")
            k.dma("g", sc["sd"], ap["sd"], "fcst")
            for e2 in range(4):
                k.dma("g", sc["expgu"][4 * e2:4 * e2 + 4], ap["expgu"][4 * e2:4 * e2 + 4], "fcst")
            k.dma("g", sc["dwcc"], ap["dwcc"], "fcst")

    def emit_ln():
        lnbf = ar("lnbf", (128, 2 * QCA), BF16)
        lnsq = ar("lnsq", (128, 2 * QCA), BF16)
        m1 = ar("m1", (128, QCA), F32)
        m2 = ar("m2", (128, QCA), F32)
        m3 = ar("m3", (128, QCA), F32)
        m4 = ar("m4", (128, QCA), F32)
        xin_tok = k.tok("dxin")
        prev_last = None
        sq_war = {0: None, 1: None}
        for qc in range(NQA):
            q0 = qc * QCA
            for ccx in range(CC):
                slot = ccx % 2
                war = ("mm", k.cnt["mm"]) if (ccx >= 2 or qc > 0) else None
                tb = k.op("s", lambda e, ccx=ccx, slot=slot, q0=q0:
                          e.copy(lnbf[:, slot * QCA:(slot + 1) * QCA],
                                 sb["x"][:, ccx * S + q0:ccx * S + q0 + QCA]),
                          waits=[xin_tok, war, sq_war[slot]], inc="act")
                tq = k.op("v", lambda e, slot=slot:
                          e.tensor_mul(lnsq[:, slot * QCA:(slot + 1) * QCA],
                                       lnbf[:, slot * QCA:(slot + 1) * QCA],
                                       lnbf[:, slot * QCA:(slot + 1) * QCA]),
                          waits=[tb, war], inc="vec")
                sq_war[slot] = tq
                k.mmw(0, lambda e, slot=slot, ccx=ccx:
                      e.matmul(P[0][:], sb["ones_bf"][:],
                               lnbf[:, slot * QCA:(slot + 1) * QCA],
                               start=(ccx == 0), stop=(ccx == CC - 1)),
                      waits=[tb], start=(ccx == 0), stop=(ccx == CC - 1))
                k.mmw(1, lambda e, slot=slot, ccx=ccx:
                      e.matmul(P[1][:], sb["ones_bf"][:],
                               lnsq[:, slot * QCA:(slot + 1) * QCA],
                               start=(ccx == 0), stop=(ccx == CC - 1)),
                      waits=[tq], start=(ccx == 0), stop=(ccx == CC - 1))
            t1 = k.prd("s", 0, lambda e: e.mul(m1[:], P[0][:], 1.0 / H), waits=[prev_last])
            t2 = k.prd("s", 1, lambda e: e.mul(m2[:], P[1][:], 1.0 / H), waits=[prev_last])
            t3 = k.op("v", lambda e: e.tensor_mul(m3[:], m1[:], m1[:]),
                      waits=[t1, prev_last], inc="vec")
            t4 = k.op("v", lambda e: e.tensor_sub(m2[:], m2[:], m3[:]), waits=[t2, t3], inc="vec")
            t5 = k.op("s", lambda e: e.activation(m3[:], m2[:], AF.Ln, bias=sb["eps_sb"][:, 0:1]),
                      waits=[t4], inc="act")
            t6 = k.op("s", lambda e: e.activation(m2[:], m3[:], AF.Exp, scale=-0.5),
                      waits=[t5], inc="act")
            t7 = k.op("v", lambda e: e.tensor_mul(m3[:], m1[:], m2[:]), waits=[t6], inc="vec")
            lastc = prev_last
            for ccx in range(CC):
                t8 = k.op("v", lambda e, ccx=ccx, q0=q0:
                          e.tensor_mul(m4[:], sb["x"][:, ccx * S + q0:ccx * S + q0 + QCA], m2[:]),
                          waits=[t6, lastc], inc="vec")
                lastc = k.op("v", lambda e, ccx=ccx, q0=q0:
                             e.tensor_sub(sb["xn"][:, ccx * S + q0:ccx * S + q0 + QCA],
                                          m4[:], m3[:]),
                             waits=[t7, t8], inc="vec")
            last = lastc
            prev_last = last
        return last

    def emit_attn(mode):
        a = mode == "a"
        QC = QCA if a else QCB
        NQ = NQA if a else NQB
        LAG = 2
        if a:
            csA = ar("csA", (128, S), F32)
            csB = ar("csB", (128, S), F32)
            mask = ar("maska", (128, 4 * QCA), BF16)
        else:
            mask = ar("maskb", (128, 4 * QCB), BF16)
        Qh = ar("Qh", (128, S), BF16)
        Kh = ar("Kh", (128, S), BF16)
        VT = ar("VT", (128, S), BF16)
        aout = ar("aout", (128, CC * S), BF16)
        nsl = 6 if a else 4
        nqk = nsl - 2
        wsl = [ar(f"w{i}", (128, 1024), BF16) for i in range(2 * nqk + 1)]
        ebuf_off = (A[0] + 63) // 64 * 64
        ebuf = ar("ebuf", (128, 6 * QC), BF16)
        embuf = ar("embuf", (128, 6 * QC), BF16)
        wo_sl = [ar_at("wo0", (128, 1024), BF16, ebuf_off),
                 ar_at("wo1", (128, 1024), BF16, ebuf_off + 2048)]
        rp1 = ar("rp1", (128, QCA), F32)
        rp2 = ar("rp2", (128, QCA), F32)
        qsq_all = ar("qsq_all", (128, 4 * QCA), BF16)
        ksq_all = ar("ksq_all", (128, 4 * QCA), BF16)
        nsb = ar("nsb", (128, QC), F32)
        dinv = ar("dinv", (33, 2 * QC), BF16)
        nacc = ar("nacc", (1, 16), F32)
        nred = ar("nred", (1, 8), F32)
        bias_sb = ar("bias_sb", (128, 2), F32)
        cst = []
        if a:
            cst.append(k.dma("y", csA[:], ap["csA"], "dcst"))
            cst.append(k.dma("y", csB[:], ap["csB"], "dcst"))
            cst.append(k.dma("y", mask[:], ap["maska"], "dcst"))
        else:
            cst.append(k.dma("y", mask[:], ap["maskb"], "dcst"))
        cst_tok = cst[-1]

        mats = (["aqm", "aqr", "akm", "akr", "avT", "aoT"] if a
                else ["bqT", "bkT", "bvT", "boT"])
        nm = 4 if a else 2
        iv = nm
        nw = nsl - 1
        wtok = {}
        wfree = {}
        cast_tok = ("awcast", k.cnt["awcast"])

        def loadw(mi, c):
            slot = (c % 2) * nqk + mi if mi < nm else 2 * nqk
            t = k.dma("y", wsl[slot][:], scw[mats[mi]][c], f"aw{slot}",
                      waits=[wfree.get(slot), cast_tok])
            wtok[(mi, c)] = (t, slot)

        for mi in range(nm + 1):
            loadw(mi, 0)

        last_xn = k.tok("vec")
        last_ln_act = k.tok("act")
        dv_toks = [None, None]
        prev_rope = None
        prev_sq = {}
        for c in range(CC):
            if c + 1 < CC:
                for mi in range(nm):
                    loadw(mi, c + 1)
            qk_done = None
            for qc in range(NQA):
                q0 = qc * QCA
                pb = 4 * (qc % 2)
                par = qc % 2
                pm = []
                for mi in range(nm):
                    wt, slot = wtok[(mi, c)]
                    wb = wsl[slot]
                    k.twait(k.takepr(pb + mi) + [wt, last_xn, last_ln_act])
                    for ccx in range(CC):
                        tk = k.mmw(pb + mi, lambda e, mi=mi, ccx=ccx, q0=q0, wb=wb, pb=pb:
                                   e.matmul(P[pb + mi][:], wb[:, ccx * 128:ccx * 128 + 128],
                                            sb["xn"][:, ccx * S + q0:ccx * S + q0 + QCA],
                                            start=(ccx == 0), stop=(ccx == CC - 1)),
                                   start=(ccx == 0), stop=(ccx == CC - 1), nowait=True)
                    pm.append(tk)
                    wfree[slot] = tk
                iq, ik = (0, 2) if a else (0, 1)
                tqs = k.prd("s", pb + iq, lambda e, q0=q0, pb=pb, iq=iq:
                            e.square(qsq_all[:, q0:q0 + QCA], P[pb + iq][:]),
                            waits=[prev_sq.get(("q", par))])
                tks = k.prd("s", pb + ik, lambda e, q0=q0, pb=pb, ik=ik:
                            e.square(ksq_all[:, q0:q0 + QCA], P[pb + ik][:]),
                            waits=[prev_sq.get(("k", par))])
                prev_sq[("q", par)] = tqs
                prev_sq[("k", par)] = tks
                if a:
                    r1 = rp1[:]
                    r2 = rp2[:]
                    t1 = k.prd("v", pb + 0, lambda e, q0=q0, pb=pb, r1=r1: e.tensor_mul(
                        r1, P[pb + 0][:], csA[:, q0:q0 + QCA]),
                        waits=[cst_tok, prev_rope, tqs])
                    t2 = k.prd("v", pb + 1, lambda e, q0=q0, pb=pb, r2=r2: e.tensor_mul(
                        r2, P[pb + 1][:], csB[:, q0:q0 + QCA]))
                    tq = k.op("v", lambda e, q0=q0, r1=r1, r2=r2: e.tensor_add(
                        Qh[:, q0:q0 + QCA], r1, r2), waits=[t1, t2], inc="vec")
                    t1 = k.prd("v", pb + 2, lambda e, q0=q0, pb=pb, r1=r1: e.tensor_mul(
                        r1, P[pb + 2][:], csA[:, q0:q0 + QCA]), waits=[tq, tks])
                    t2 = k.prd("v", pb + 3, lambda e, q0=q0, pb=pb, r2=r2: e.tensor_mul(
                        r2, P[pb + 3][:], csB[:, q0:q0 + QCA]))
                    tk2 = k.op("v", lambda e, q0=q0, r1=r1, r2=r2: e.tensor_add(
                        Kh[:, q0:q0 + QCA], r1, r2), waits=[t1, t2], inc="vec")
                    prev_rope = tk2
                else:
                    tq = k.prd("s", pb + 0, lambda e, q0=q0, pb=pb: e.copy(
                        Qh[:, q0:q0 + QCA], P[pb + 0][:]), waits=[tqs])
                    tk2 = k.prd("s", pb + 1, lambda e, q0=q0, pb=pb: e.copy(
                        Kh[:, q0:q0 + QCA], P[pb + 1][:]), waits=[tks])
                qk_done = tk2
            nmt = {}
            for qc in range(NQA):
                for hh in range(2):
                    src_q = qsq_all[64 * hh:64 * hh + 64, qc * QCA:(qc + 1) * QCA]
                    src_k = ksq_all[64 * hh:64 * hh + 64, qc * QCA:(qc + 1) * QCA]
                    bq = 2 * hh
                    bk = 2 * hh + 1
                    k.mmw(bq, lambda e, src_q=src_q, hh=hh, bq=bq: e.matmul(
                        P[bq][0:1, :], sb["ones_bf"][64 * hh:64 * hh + 64, 0:1], src_q,
                        start=True, stop=True, tile_position=(64 * hh, 0)),
                        waits=[prev_sq[("q", qc % 2)]])
                    k.mmw(bk, lambda e, src_k=src_k, hh=hh, bk=bk: e.matmul(
                        P[bk][0:1, :], sb["ones_bf"][64 * hh:64 * hh + 64, 0:1], src_k,
                        start=True, stop=True, tile_position=(64 * hh, 0)),
                        waits=[prev_sq[("k", qc % 2)]])
                    k.prd("v", bq, lambda e, qc=qc, hh=hh, bq=bq: e.tensor_reduce(
                        nacc[0:1, 8 * hh + qc:8 * hh + qc + 1], P[bq][0:1, :], AX.X, ALU.max))
                    nmt[(hh, qc)] = k.prd("v", bk, lambda e, qc=qc, hh=hh, bk=bk:
                                          e.tensor_reduce(
                        nacc[0:1, 8 * hh + 4 + qc:8 * hh + 5 + qc], P[bk][0:1, :],
                        AX.X, ALU.max))
            rr4 = {}
            rprev = nmt[(1, NQA - 1)]
            for hh in range(2):
                rr1 = k.op("v", lambda e, hh=hh: e.tensor_reduce(
                    nred[0:1, 4 * hh:4 * hh + 1], nacc[0:1, 8 * hh:8 * hh + 4],
                    AX.X, ALU.max), waits=[rprev], inc="vec")
                rr2 = k.op("v", lambda e, hh=hh: e.tensor_reduce(
                    nred[0:1, 4 * hh + 1:4 * hh + 2], nacc[0:1, 8 * hh + 4:8 * hh + 8],
                    AX.X, ALU.max), waits=[rr1], inc="vec")
                rr3 = k.op("v", lambda e, hh=hh: e.tensor_add(
                    nred[0:1, 4 * hh + 2:4 * hh + 3], nred[0:1, 4 * hh:4 * hh + 1],
                    nred[0:1, 4 * hh + 1:4 * hh + 2]), waits=[rr2], inc="vec")
                rprev = rr4[hh] = k.op("v", lambda e, hh=hh: e.tensor_scalar_mul(
                    nred[0:1, 4 * hh + 3:4 * hh + 4], nred[0:1, 4 * hh + 2:4 * hh + 3],
                    -1.0 / 16.0), waits=[rr3], inc="vec")
            for hh in range(2):
                k.mmw(0, lambda e, hh=hh: e.matmul(
                    P[0][:, hh:hh + 1], sb["onesf"][0:1, 0:128],
                    nred[0:1, 4 * hh + 3:4 * hh + 4], start=True, stop=True),
                    waits=[rr4[hh]], start=(hh == 0), stop=(hh == 1))
            tbias = k.prd("s", 0, lambda e: e.copy(bias_sb[:], P[0][:, 0:2]))
            wt_v, slot_v = wtok[(iv, c)]
            wb_v = wsl[slot_v]
            vt_toks = []
            for tt in range(16):
                bnk = tt % 4
                k.twait(k.takepr(bnk) + [wt_v])
                for ccx in range(CC):
                    k.mmw(bnk, lambda e, tt=tt, ccx=ccx, wb_v=wb_v, bnk=bnk:
                          e.matmul(P[bnk][:, 0:128],
                                   sb["xn"][:, ccx * S + tt * 128:ccx * S + tt * 128 + 128],
                                   wb_v[:, ccx * 128:ccx * 128 + 128],
                                   start=(ccx == 0), stop=(ccx == CC - 1)),
                          start=(ccx == 0), stop=(ccx == CC - 1), nowait=True)
                ev = k.prd("v", bnk, lambda e, tt=tt, bnk=bnk: e.tensor_copy(
                    VT[:, tt * 128:tt * 128 + 128], P[bnk][:, 0:128]))
                vt_toks.append(ev)
            wfree[slot_v] = vt_toks[-1]
            vt_done = vt_toks[-1]
            if c + 1 < CC:
                loadw(iv, c + 1)

            econs = {}
            emcons = {}
            for qc in range(NQ):
                q0 = qc * QC
                if a:
                    kts = list(range(16))
                    pv_lo, pv_hi = 0, 4 * qc + 4
                    str_lo = 4 * qc
                else:
                    kts = [t_ for t_ in range(2 * qc - 2, 2 * qc + 2) if t_ >= 0]
                    pv_lo, pv_hi = kts[0], kts[-1] + 1
                    str_lo = -10**9
                nkt = len(kts)
                npv = pv_hi - pv_lo
                state = {}
                pv_cnt = [0]
                d_cnt = [0]

                def emit_s_exp(j):
                    kt = kts[j]
                    w = []
                    for hh in range(2):
                        w += k.takepr(2 * hh + j % 2)
                    if j == 0:
                        w += [qk_done, dv_toks[0], dv_toks[1], tbias]
                    k.twait(w)
                    smm = []
                    for hh in range(2):
                        bnk = 2 * hh + j % 2
                        smm.append(k.mmw(bnk, lambda e, hh=hh, kt=kt, q0=q0, bnk=bnk:
                                   e.matmul(P[bnk][:, 0:QC],
                                            Kh[64 * hh:64 * hh + 64, kt * 128:kt * 128 + 128],
                                            Qh[64 * hh:64 * hh + 64, q0:q0 + QC],
                                            start=True, stop=True),
                                   nowait=True))
                    for hh in range(2):
                        bnk = 2 * hh + j % 2
                        sl4 = (hh * 3 + j % 3) * QC
                        eb = ebuf[:, sl4:sl4 + QC]
                        texp = k.prd("s", bnk, lambda e, bnk=bnk, eb=eb, hh=hh:
                                     e.activation(eb, P[bnk][:, 0:QC], AF.Exp,
                                                  bias=bias_sb[:, hh:hh + 1], scale=0.125),
                                     waits=[tbias, econs.get((hh, j % 3))])
                        use_mask = ((a and str_lo <= kt < pv_hi) or (not a))
                        pv_use, tmask = eb, texp
                        if use_mask and pv_lo <= kt < pv_hi:
                            mb2 = embuf[:, sl4:sl4 + QC]
                            r = (kt - 4 * qc) if a else (kt - 2 * qc + 2)
                            tmask = k.op("v", lambda e, mb2=mb2, eb=eb, r=r:
                                         e.tensor_mul(mb2, eb, mask[:, r * QC:r * QC + QC]),
                                         waits=[texp, cst_tok, emcons.get((hh, j % 3))],
                                         inc="vec")
                            pv_use = mb2
                        state[(j, hh)] = (texp, tmask, pv_use, eb)

                def emit_dpv(j):
                    kt = kts[j]
                    first_d = d_cnt[0] == 0
                    last_d = d_cnt[0] == nkt - 1
                    in_pv = pv_lo <= kt < pv_hi
                    first_pv = in_pv and pv_cnt[0] == 0
                    last_pv = in_pv and pv_cnt[0] == npv - 1
                    w = []
                    if first_d:
                        w += k.takepr(6) + k.takepr(7)
                    if first_pv:
                        w += k.takepr(4) + k.takepr(5)
                    for hh in range(2):
                        texp, tmask, pv_use, eb = state[(j, hh)]
                        w.append(tmask if not a else texp)
                        if in_pv:
                            w.append(tmask)
                    if first_pv:
                        w.append(vt_done)
                    k.twait(w)
                    for hh in range(2):
                        texp, tmask, pv_use, eb = state[(j, hh)]
                        d_src = eb if a else pv_use
                        dmm = k.mmw(6 + hh, lambda e, hh=hh, d_src=d_src,
                                    st=first_d, sp=last_d:
                                    e.matmul(P[6 + hh][32 * hh:32 * hh + 1, 0:QC],
                                             sb["ones_bf"][:, 0:1], d_src,
                                             start=st, stop=sp,
                                             tile_position=(0, 32 * hh)),
                                    start=first_d, stop=last_d, nowait=True)
                        if a:
                            econs[(hh, j % 3)] = dmm
                        else:
                            econs[(hh, j % 3)] = tmask
                    d_cnt[0] += 1
                    if in_pv:
                        for hh in range(2):
                            texp, tmask, pv_use, eb = state[(j, hh)]
                            pvmm = k.mmw(4 + hh, lambda e, hh=hh, kt=kt, pv_use=pv_use,
                                         st=first_pv, sp=last_pv:
                                         e.matmul(P[4 + hh][64 * hh:64 * hh + 64, 0:QC],
                                                  VT[:, kt * 128 + 64 * hh:kt * 128 + 64 * hh + 64],
                                                  pv_use, start=st, stop=sp,
                                                  tile_position=(0, 64 * hh)),
                                         start=first_pv, stop=last_pv, nowait=True)
                            if pv_use is eb:
                                econs[(hh, j % 3)] = pvmm
                            else:
                                emcons[(hh, j % 3)] = pvmm
                        pv_cnt[0] += 1
                    del state[(j, 0)], state[(j, 1)]

                for j in range(nkt):
                    emit_s_exp(j)
                    if j >= LAG:
                        emit_dpv(j - LAG)
                for j in range(max(0, nkt - LAG), nkt):
                    emit_dpv(j)
                def _recipa(e):
                    with nc.allow_low_precision(reason="bf16 inverse-denominator, margin-checked"):
                        return e.reciprocal(dinv[0:1, 0:QC], P[6][0:1, 0:QC])

                def _recipb(e):
                    with nc.allow_low_precision(reason="bf16 inverse-denominator, margin-checked"):
                        return e.reciprocal(dinv[32:33, QC:2 * QC], P[7][32:33, 0:QC])
                dra = k.prd("v", 6, _recipa)
                drb = k.prd("v", 7, _recipb)
                k.twait(k.takepr(6) + k.takepr(7) + [dra, drb])
                k.mmw(6, lambda e: e.matmul(
                    P[6][:, 0:QC], sb["ones_bf"][0:1, 0:128], dinv[0:1, 0:QC],
                    start=True, stop=True), nowait=True)
                k.mmw(7, lambda e: e.matmul(
                    P[7][:, 0:QC], sb["ones_bf"][32:33, 0:128], dinv[32:33, QC:2 * QC],
                    start=True, stop=True), nowait=True)
                ne1 = k.prd("v", 4, lambda e: e.tensor_copy(nsb[0:64, 0:QC], P[4][0:64, 0:QC]),
                            waits=[dv_toks[0]])
                ne2 = k.prd("v", 5, lambda e: e.tensor_copy(nsb[64:128, 0:QC], P[5][64:128, 0:QC]),
                            waits=[dv_toks[1]])
                dv_toks[0] = k.prd("v", 6, lambda e, c=c, q0=q0: e.tensor_mul(
                    aout[0:64, c * S + q0:c * S + q0 + QC], nsb[0:64, 0:QC], P[6][0:64, 0:QC]),
                    waits=[ne1])
                dv_toks[1] = k.prd("v", 7, lambda e, c=c, q0=q0: e.tensor_mul(
                    aout[64:128, c * S + q0:c * S + q0 + QC], nsb[64:128, 0:QC],
                    P[7][64:128, 0:QC]),
                    waits=[ne2])
        last_aout = dv_toks[1]

        for fc in range(CC):
            slot = 20 + fc % 2
            t_wo = k.dma("y", wo_sl[fc % 2][:], scw[mats[nw]][fc], f"aw{slot}",
                         waits=[wfree.get(slot), cast_tok, last_aout])
            for qc in range(NQA):
                q0 = qc * QCA
                pp = 2 + (fc * NQA + qc) % 2
                k.twait(k.takepr(pp) + [t_wo, last_aout])
                for dc in range(CC):
                    k.mmw(pp, lambda e, dc=dc, q0=q0, fc=fc, pp=pp:
                          e.matmul(P[pp][:], wo_sl[fc % 2][:, dc * 128:dc * 128 + 128],
                                   aout[:, dc * S + q0:dc * S + q0 + QCA],
                                   start=(dc == 0), stop=(dc == CC - 1)),
                          start=(dc == 0), stop=(dc == CC - 1), nowait=True)
                tadd = k.prd("v", pp, lambda e, fc=fc, q0=q0, pp=pp: e.tensor_add(
                    sb["x"][:, fc * S + q0:fc * S + q0 + QCA],
                    sb["x"][:, fc * S + q0:fc * S + q0 + QCA], P[pp][:]))
            wfree[slot] = tadd
        return tadd

    def emit_ffn():
        FLV = int(os.environ.get("FFN_LV", "5"))
        esel_sb = ar("esel", (NE, NE * 128), BF16)
        cbias_sb = ar("cbias", (128, 256), F32)
        rw_sb = ar("rw", (128, 128), BF16)
        comb_fm = ar("comb_fm", (NE, S), BF16)
        hh = ar("hh", (128, 48 * QCA), BF16)
        hh_off = int(hh.handle.offset) if hasattr(hh, "handle") else None
        hh_base = A[0] - 48 * QCA * 2
        sco_fm = ar_at("sco_fm", (NE, S), F32, hh_base)
        scT = ar_at("scT", (128, 256), F32, hh_base + 8192)
        sbT = ar_at("sbT", (128, 256), F32, hh_base + 8192 + 1024)
        msk = ar_at("msk", (128, 256), F32, hh_base + 8192 + 2048)
        cnt = ar_at("cnt", (128, 256), F32, hh_base + 8192 + 3072)
        ctm = ar_at("ctm", (128, 256), F32, hh_base + 8192 + 4096)
        rb = hh_base + 8192 + 5120
        gs = ar_at("gs", (128, 64), F32, rb)
        gct = ar_at("gct", (128, 64), F32, rb + 256)
        gtm = ar_at("gtm", (128, 64), F32, rb + 512)
        gm = ar_at("gm", (128, 64), F32, rb + 768)
        s1 = ar_at("s1", (128, 64), F32, rb + 1024)
        s2 = ar_at("s2", (128, 64), F32, rb + 1280)
        s3 = ar_at("s3", (128, 64), F32, rb + 1536)
        wsum = ar_at("wsum", (128, 16), F32, rb + 1792)
        comb_tm = ar_at("comb_tm", (128, 256), F32, rb + 1856)
        silu_t = [ar(f"silu_t{i}", (128, QCA), BF16) for i in range(2)]
        t2b = [ar(f"t2b{i}", (128, QCA), BF16) for i in range(2)]
        sgsu_sl = [ar(f"sgsu{i}", (128, 2048), BF16) for i in range(2)]
        sd_sl = [ar(f"sd{i}", (128, 2048), BF16) for i in range(2)]
        dw_sl = [ar(f"dw{i}", (128, 4096), BF16) for i in range(2)]
        eg_sl = [ar(f"eg{i}", (128, 4096), BF16) for i in range(2)]

        k.dma("y", esel_sb[:], ap["esel"], "dcst")
        tcs = k.dma("y", cbias_sb[:], ap["cbias"], "dcst")
        trw = k.dma("g", rw_sb[:], ap["rwT"], "gcst")
        cast_done = k.tok("fcst")
        last_xn = k.tok("vec")
        last_ln_act = k.tok("act")

        for qc in range(NQA):
            q0 = qc * QCA
            for ccx in range(CC):
                k.mmw(6, lambda e, ccx=ccx, q0=q0:
                      e.matmul(P[6][0:16, :], rw_sb[:, ccx * 16:ccx * 16 + 16],
                               sb["xn"][:, ccx * S + q0:ccx * S + q0 + QCA],
                               start=(ccx == 0), stop=(ccx == CC - 1)),
                      waits=[trw, last_xn, last_ln_act],
                      start=(ccx == 0), stop=(ccx == CC - 1))
            tsig = k.prd("s", 6, lambda e, q0=q0: e.activation(
                sco_fm[0:16, q0:q0 + QCA], P[6][0:16, :], AF.Sigmoid))
        for tt in range(16):
            k.mmw(7, lambda e, tt=tt: e.transpose(
                P[7][:, tt * 16:tt * 16 + 16], sco_fm[0:16, tt * 128:tt * 128 + 128],
                sb["identf_sb"][0:16, 0:16]),
                waits=[tsig, ident_tok[0]], start=(tt == 0), stop=(tt == 15))
        v0 = k.prd("v", 7, lambda e: e.tensor_copy(scT[:], P[7][:, 0:256]))
        vt = k.op("v", lambda e: e.tensor_add(sbT[:], scT[:], cbias_sb[:]),
                  waits=[v0, tcs], inc="vec")

        def re(apx):
            return apx.rearrange("p (t e) -> p t e", e=16)
        sb4 = sbT[:].rearrange("p (t g j) -> p t g j", g=4, j=4)
        g4 = gs[:].rearrange("p (t g) -> p t g", g=4)
        s1r = s1[:].rearrange("p (t g) -> p t g", g=4)
        s2r = s2[:].rearrange("p (t g) -> p t g", g=4)
        s3r = s3[:].rearrange("p (t g) -> p t g", g=4)
        gmr = gm[:].rearrange("p (t g) -> p t g", g=4)
        gcr = gct[:].rearrange("p (t g) -> p t g", g=4)
        gtr = gtm[:].rearrange("p (t g) -> p t g", g=4)

        def vop(fn):
            nonlocal vt
            vt = k.op("v", fn, waits=[vt], inc="vec")
            return vt
        vop(lambda e: e.tensor_tensor(s1r, sb4[:, :, :, 0], sb4[:, :, :, 1], ALU.max))
        vop(lambda e: e.tensor_tensor(s2r, sb4[:, :, :, 0], sb4[:, :, :, 1], ALU.min))
        vop(lambda e: e.tensor_tensor(s3r, sb4[:, :, :, 2], sb4[:, :, :, 3], ALU.max))
        vop(lambda e: e.tensor_tensor(gmr, sb4[:, :, :, 2], sb4[:, :, :, 3], ALU.min))
        vop(lambda e: e.tensor_tensor(gcr, s1r, s3r, ALU.max))
        vop(lambda e: e.tensor_tensor(s1r, s1r, s3r, ALU.min))
        vop(lambda e: e.tensor_tensor(s2r, s2r, gmr, ALU.max))
        vop(lambda e: e.tensor_tensor(s1r, s1r, s2r, ALU.max))
        vop(lambda e: e.tensor_tensor(g4, gcr, s1r, ALU.add))
        vop(lambda e: e.memset(gct[:], 0.0))
        for dd in range(1, 4):
            Aap = g4[:, :, 0:4 - dd]
            Bap = g4[:, :, dd:4]
            cA = gcr[:, :, 0:4 - dd]
            cB = gcr[:, :, dd:4]
            tmp = gtr[:, :, 0:4 - dd]
            vop(lambda e, A=Aap, B=Bap, t=tmp: e.tensor_tensor(t, A, B, ALU.is_ge))
            vop(lambda e, t=tmp, c2=cB: e.tensor_tensor(c2, c2, t, ALU.add))
            vop(lambda e, A=Aap, B=Bap, t=tmp: e.tensor_tensor(t, B, A, ALU.is_gt))
            vop(lambda e, t=tmp, c2=cA: e.tensor_tensor(c2, c2, t, ALU.add))
        vop(lambda e: e.tensor_single_scalar(gm[:], gct[:], 2.0, ALU.is_lt))
        mr4 = msk[:].rearrange("p (t g j) -> p t g j", g=4, j=4)
        for j in range(4):
            vop(lambda e, j=j: e.scalar_tensor_tensor(
                mr4[:, :, :, j], sb4[:, :, :, j], 4.0, gmr, ALU.add, ALU.mult))
        vop(lambda e: e.memset(cnt[:], 0.0))
        mre = re(msk[:])
        cre = re(cnt[:])
        tre = re(ctm[:])
        for dd in range(1, 16):
            Aap = mre[:, :, 0:16 - dd]
            Bap = mre[:, :, dd:16]
            cA = cre[:, :, 0:16 - dd]
            cB = cre[:, :, dd:16]
            tmp = tre[:, :, 0:16 - dd]
            vop(lambda e, A=Aap, B=Bap, t=tmp: e.tensor_tensor(t, A, B, ALU.is_ge))
            vop(lambda e, t=tmp, c2=cB: e.tensor_tensor(c2, c2, t, ALU.add))
            vop(lambda e, A=Aap, B=Bap, t=tmp: e.tensor_tensor(t, B, A, ALU.is_gt))
            vop(lambda e, t=tmp, c2=cA: e.tensor_tensor(c2, c2, t, ALU.add))
        vop(lambda e: e.tensor_single_scalar(msk[:], cnt[:], 4.0, ALU.is_lt))
        vop(lambda e: e.tensor_mul(ctm[:], scT[:], msk[:]))
        vop(lambda e: e.tensor_reduce(wsum[:], re(ctm[:]), AX.X, ALU.add))
        vop(lambda e: e.tensor_scalar_add(wsum[:], wsum[:], 1e-20))
        vop(lambda e: e.reciprocal(wsum[:], wsum[:]))
        winb = wsum[:].rearrange("p (t e) -> p t e", e=1).broadcast_to((128, 16, 16))
        vop(lambda e: e.tensor_tensor(re(comb_tm[:]), re(ctm[:]), winb, ALU.mult))
        cevac = None
        for b4 in range(4):
            for j4 in range(4):
                tt = 4 * b4 + j4
                k.mmw(7, lambda e, tt=tt, j4=j4: e.transpose(
                    P[7][0:16, j4 * 128:j4 * 128 + 128],
                    comb_tm[:, tt * 16:tt * 16 + 16], sb["identf_sb"][:]),
                    waits=[vt], start=(j4 == 0), stop=(j4 == 3))
            cevac = k.prd("s", 7, lambda e, b4=b4: e.copy(
                comb_fm[0:16, b4 * 512:b4 * 512 + 512], P[7][0:16, 0:512]))
        comb_done = cevac

        sl_free = {0: None, 1: None}
        ed_free = {0: None, 1: None}
        dwf = {0: None, 1: None}
        sdf = {0: None, 1: None}
        st_free = {0: None, 1: None}
        t2_free = {0: None, 1: None}
        hh_tok = {}
        down_done = None
        res_tok = None
        for qc in range(NQA):
            q0 = qc * QCA
            if FLV < 2:
                continue
            for hc in range(16):
                sl = hc % 2
                tl = k.dma("y", sgsu_sl[sl][:], sc["sgsu"][hc], f"fs{sl}",
                           waits=[cast_done, sl_free[sl]])
                pp = hc % 2
                for ccx in range(CC):
                    k.mmw(pp, lambda e, ccx=ccx, q0=q0, sl=sl, pp=pp:
                          e.matmul(P[pp][:], sgsu_sl[sl][:, ccx * 128:ccx * 128 + 128],
                                   sb["xn"][:, ccx * S + q0:ccx * S + q0 + QCA],
                                   start=(ccx == 0), stop=(ccx == CC - 1)),
                          waits=[tl, last_xn, last_ln_act],
                          start=(ccx == 0), stop=(ccx == CC - 1))
                for ccx in range(CC):
                    tu = k.mmw(2 + pp, lambda e, ccx=ccx, q0=q0, sl=sl, pp=pp:
                               e.matmul(P[2 + pp][:],
                                        sgsu_sl[sl][:, 1024 + ccx * 128:1024 + ccx * 128 + 128],
                                        sb["xn"][:, ccx * S + q0:ccx * S + q0 + QCA],
                                        start=(ccx == 0), stop=(ccx == CC - 1)),
                               waits=[tl],
                               start=(ccx == 0), stop=(ccx == CC - 1))
                ts = k.prd("s", pp, lambda e, pp=pp: e.activation(
                    silu_t[pp][:], P[pp][:], AF.Silu), waits=[st_free[pp]])
                th = k.prd("v", 2 + pp, lambda e, hc=hc, pp=pp: e.tensor_mul(
                    hh[:, hc * QCA:hc * QCA + QCA], silu_t[pp][:], P[2 + pp][:]),
                    waits=[ts, down_done])
                st_free[pp] = th
                hh_tok[hc] = th
                sl_free[sl] = tu
            for ee in range(NE):
                if FLV < 3:
                    break
                sl = ee % 2
                tl = k.dma("y", eg_sl[sl][:], sc["expgu"][ee], f"fe{sl}",
                           waits=[cast_done, ed_free[sl]])
                tcb = k.mmw(4, lambda e, ee=ee, q0=q0: e.matmul(
                    P[4][:], esel_sb[:, ee * 128:ee * 128 + 128],
                    comb_fm[0:16, q0:q0 + QCA], start=True, stop=True),
                    waits=[comb_done])
                for h2 in range(2):
                    pp = h2
                    for ccx in range(CC):
                        k.mmw(pp, lambda e, ccx=ccx, q0=q0, sl=sl, h2=h2, pp=pp:
                              e.matmul(P[pp][:],
                                       eg_sl[sl][:, (h2 * 8 + ccx) * 128:(h2 * 8 + ccx) * 128 + 128],
                                       sb["xn"][:, ccx * S + q0:ccx * S + q0 + QCA],
                                       start=(ccx == 0), stop=(ccx == CC - 1)),
                              waits=[tl], start=(ccx == 0), stop=(ccx == CC - 1))
                    for ccx in range(CC):
                        tu = k.mmw(2 + pp, lambda e, ccx=ccx, q0=q0, sl=sl, h2=h2, pp=pp:
                                   e.matmul(P[2 + pp][:],
                                            eg_sl[sl][:, 2048 + (h2 * 8 + ccx) * 128:2048 + (h2 * 8 + ccx) * 128 + 128],
                                            sb["xn"][:, ccx * S + q0:ccx * S + q0 + QCA],
                                            start=(ccx == 0), stop=(ccx == CC - 1)),
                                   waits=[tl], start=(ccx == 0), stop=(ccx == CC - 1))
                    ts = k.prd("s", pp, lambda e, pp=pp: e.activation(
                        silu_t[pp][:], P[pp][:], AF.Silu), waits=[st_free[pp]])
                    tm = k.prd("v", 2 + pp, lambda e, pp=pp: e.tensor_mul(
                        t2b[pp][:], silu_t[pp][:], P[2 + pp][:]),
                        waits=[ts, t2_free[pp]])
                    st_free[pp] = tm
                    th = k.prd("v", 4, lambda e, ee=ee, h2=h2, pp=pp: e.tensor_mul(
                        hh[:, (16 + 2 * ee + h2) * QCA:(16 + 2 * ee + h2 + 1) * QCA],
                        t2b[pp][:], P[4][:]), waits=[tm, down_done])
                    t2_free[pp] = th
                    hh_tok[16 + 2 * ee + h2] = th
                ed_free[sl] = tu
            for ccx in range(CC):
                if FLV < 4:
                    break
                sl = ccx % 2
                tsd = k.dma("y", sd_sl[sl][:], sc["sd"][ccx], f"fd{sl}",
                            waits=[cast_done, sdf[sl]])
                tdw = k.dma("y", dw_sl[sl][:], sc["dwcc"][ccx], f"fw{sl}",
                            waits=[cast_done, dwf[sl]])
                pp = 5 + ccx % 2
                shstop = FLV < 5
                for hc in range(16):
                    tk = k.mmw(pp, lambda e, hc=hc, sl=sl, pp=pp,
                               sp=(shstop and hc == 15):
                               e.matmul(P[pp][:], sd_sl[sl][:, hc * 128:hc * 128 + 128],
                                        hh[:, hc * QCA:hc * QCA + QCA],
                                        start=(hc == 0), stop=sp),
                               waits=[tsd, hh_tok[hc]],
                               start=(hc == 0), stop=(shstop and hc == 15))
                for ee in range(NE):
                    if FLV < 5:
                        break
                    for h2 in range(2):
                        idx = 2 * ee + h2
                        tk = k.mmw(pp, lambda e, idx=idx, sl=sl, pp=pp, sp=(idx == 31):
                                   e.matmul(P[pp][:],
                                            dw_sl[sl][:, idx * 128:idx * 128 + 128],
                                            hh[:, (16 + idx) * QCA:(16 + idx + 1) * QCA],
                                            start=False, stop=sp),
                                   waits=[tdw, hh_tok[16 + idx]],
                                   start=False, stop=(idx == 31))
                def _res(e, ccx=ccx, q0=q0, pp=pp, qc=qc):
                    inst = e.tensor_add(
                        sb["x"][:, ccx * S + q0:ccx * S + q0 + QCA],
                        sb["x"][:, ccx * S + q0:ccx * S + q0 + QCA], P[pp][:])
                    inst.annotate(f"res qc{qc} ccx{ccx} pp{pp}")
                    return inst
                res_tok = k.prd("v", pp, _res)
                sdf[sl] = tk
                dwf[sl] = tk
                down_done = res_tok
        return res_tok

    def emit_store():
        for ccx in range(CC):
            k.dma("y", yout[ccx], sb["x"][:, ccx * S:(ccx + 1) * S], "dst")
        k.op("y", lambda e: None, waits=[("dst", 16 * CC)])

    emit_load()
    k.barrier()
    for st in stages:
        ar_reset()
        if st.startswith("ln"):
            emit_ln()
        elif st == "attn_a":
            emit_attn("a")
        elif st == "attn_b":
            emit_attn("b")
        else:
            emit_ffn()
        k.barrier()
    emit_store()

    for name in list(k.cnt):
        k.sems[name] = nc.alloc_semaphore(name)

    with nc.Block() as block:
        @block.tensor
        def _(e):
            for f in k.prog["t"]:
                f(e)

        @block.vector
        def _(e):
            for f in k.prog["v"]:
                f(e)

        @block.scalar
        def _(e):
            for f in k.prog["s"]:
                f(e)

        @block.gpsimd
        def _(e):
            for f in k.prog["g"]:
                f(e)

        @block.sync
        def _(e):
            for f in k.prog["y"]:
                f(e)
    return nc


def kernel(**inputs):
    return _run(inputs, list(ALL_STAGES))


def _run(inputs, stages, cores=None):
    hp = host_prep(inputs)
    x = np.asarray(inputs["L0_attn_a_x"])
    B = x.shape[0]
    if cores is None:
        cores = list(range(B))
    nc = build_nc(tuple(stages))
    in_maps = []
    for b in cores:
        m = dict(hp)
        m["xin"] = np.ascontiguousarray(x[b].T.reshape(CC, 128, S))
        in_maps.append(m)
    res = run_bass_kernel_spmd(nc, in_maps, core_ids=list(range(len(cores))))
    outs = []
    for r in res.results:
        y = np.asarray(r["y"]).reshape(H, S)
        outs.append(np.ascontiguousarray(y.T))
    return np.stack(outs, axis=0)
